# version 09 (final)
# speedup vs baseline: 1.2008x; 1.0911x over previous
import math
from contextlib import ExitStack

import numpy as np
import concourse.bass as bass
import concourse.mybir as mybir
from concourse.bass_utils import run_bass_kernel_spmd

F32 = mybir.dt.float32
BF16 = mybir.dt.bfloat16
I32 = mybir.dt.int32
AF = mybir.ActivationFunctionType
ALU = mybir.AluOpType
AX = mybir.AxisListType

B, T, D = 2, 8192, 1024
NCORE = 8
H_ALL, DH = 16, 64
NH = 4
SLAB = 2048
NMC = 8
FF = 4096
NFC = 32
NE = 8
SCALE = 0.125
C_MAGIC = float(np.float32(1.5 * 2 ** 23))
EPS = 1e-5

def _pwl_coeffs():
    tb = [0.25 * j for j in range(17)]
    vals = [0.5 * t * (1.0 + math.erf(t / math.sqrt(2.0))) - 0.5 * t for t in tb]
    t_gap = 4.0 + 1.0 / 256.0
    knots = tb + [t_gap]
    vals = vals + [t_gap / 2.0]
    slopes = [(vals[i + 1] - vals[i]) / (knots[i + 1] - knots[i])
              for i in range(len(knots) - 1)] + [0.5]
    m0 = slopes[0]
    bks = [float(knots[i]) for i in range(1, len(slopes))]
    dms = [float(slopes[i] - slopes[i - 1]) for i in range(1, len(slopes))]
    const = -sum(d * b for d, b in zip(dms, bks)) + vals[0]
    return float(m0), bks, dms, float(const)

PWL_M0, PWL_BKS, PWL_DMS, PWL_CONST = _pwl_coeffs()

def _aux_arrays():
    import ml_dtypes
    ident = np.eye(128, dtype=np.float32)
    ones_col = np.ones((128, 1), np.float32)
    ones_row = np.ones((1, 128), np.float32)
    masks = np.zeros((128, 4 * 512), np.float32)
    kk = np.arange(128)[:, None]
    qq = np.arange(512)[None, :]
    for j in range(4):
        masks[:, 512 * j:512 * (j + 1)] = (128 * j + kk <= qq)
    return ident, ones_col, ones_row, masks.astype(ml_dtypes.bfloat16)

_DBG_SPECS = {
    "xn_slab": ([D, SLAB], BF16),
    "gate": ([8, SLAB], F32),
    "q01": ([128, T], BF16),
    "k01": ([128, T], BF16),
    "vp0": ([128, 65 * 64], BF16),
    "mrow": ([4, T], BF16),
    "osl": ([D, SLAB], BF16),
    "x1": ([D, SLAB], F32),
    "xn1": ([D, SLAB], BF16),
    "kv1": ([D, SLAB], F32),
    "wkv": ([D, SLAB], BF16),
    "k2": ([D, SLAB], F32),
    "xk": ([512, SLAB], BF16),
    "xnf": ([512, SLAB], F32),
    "sin": ([128, 16], F32),
    "hfull": ([FF, SLAB], BF16),
    "hff": ([D, SLAB], F32),
    "wfm": ([8, SLAB], F32),
}

class _Sem:
    def __init__(self, nc, ctx, name):
        self.h = ctx.enter_context(nc.semaphore(name))
        self.n = 0

    def inc(self, ins, k=1):
        ins.then_inc(self.h, k)
        self.n += k
        return ins


def build_nc(upto="full", debug=()):
    nc = bass.Bass("TRN2", target_bir_lowering=False, debug=False)

    din = {}
    def dram_in(name, shape, dt=F32):
        din[name] = nc.dram_tensor(name, list(shape), dt, kind="ExternalInput")
        return din[name]

    xseq = dram_in("xseq", [T, D])
    xslab = dram_in("xslab", [SLAB, D])
    wq_my = dram_in("wq_my", [NH * DH, D])
    wk_my = dram_in("wk_my", [NH * DH, D])
    wv_my = dram_in("wv_my", [NH * DH, D])
    wo0 = dram_in("wo0", [D, D])
    wg = dram_in("wg", [D, NE])
    bg_ = dram_in("bg_", [NE, 1])
    w1moe = dram_in("w1moe", [NE, D, FF])
    b1moe = dram_in("b1moe", [NE, 32, 128])
    w2moe = dram_in("w2moe", [NE, FF, D])
    b2moe = dram_in("b2moe", [NE, D])
    l1wk = dram_in("l1wk", [D, D]); l1wv = dram_in("l1wv", [D, D])
    l1wr = dram_in("l1wr", [D, D]); l1wo = dram_in("l1wo", [D, D])
    sdec = dram_in("sdec", [8, 128]); sfirst = dram_in("sfirst", [8, 128])
    mixk = dram_in("mixk", [8, 128]); mixv = dram_in("mixv", [8, 128])
    mixr = dram_in("mixr", [8, 128])
    fw1 = dram_in("fw1", [D, FF]); fb1 = dram_in("fb1", [32, 128])
    fw2 = dram_in("fw2", [FF, D]); fb2 = dram_in("fb2", [8, 128])
    ident_in = dram_in("ident", [128, 128])
    onesc_in = dram_in("onesc", [128, 1])
    onesr_in = dram_in("onesr", [1, 128])
    masks_in = dram_in("masks", [128, 4 * 512], BF16)
    selL_in = dram_in("selL", [8, 1]); selR_in = dram_in("selR", [8, 1])
    selA_in = dram_in("selA", [8, 1]); selB_in = dram_in("selB", [8, 1])
    qpow_in = dram_in("qpow", [1, 1]); spow_in = dram_in("spow", [1, 1])
    soff_in = dram_in("soff", [1, 1], I32)
    esel_in = dram_in("esel", [8, 8 * 128])

    cc1_in = nc.dram_tensor("cc1_in", [NH * DH, T], BF16, kind="Internal")
    cc1_out = nc.dram_tensor("cc1_out", [D, T], BF16, kind="Internal")
    cc3_in = nc.dram_tensor("cc3_in", [2, D], F32, kind="Internal")
    cc3_out = nc.dram_tensor("cc3_out", [8, D], F32, kind="Internal")
    ccs_in = nc.dram_tensor("ccs_in", [2, D], F32, kind="Internal")
    ccs_out = nc.dram_tensor("ccs_out", [8, D], F32, kind="Internal")
    x1spill = nc.dram_tensor("x1spill", [D, SLAB], F32, kind="Internal")

    out_slab = nc.dram_tensor("out_slab", [SLAB, D], F32, kind="ExternalOutput")
    dbg = {}
    for name in debug:
        shape, dt = _DBG_SPECS[name]
        dbg[name] = nc.dram_tensor("dbg_" + name, list(shape), dt,
                                   kind="ExternalOutput")

    groups = [[0, 1, 2, 3], [4, 5, 6, 7]]

    with ExitStack() as ctx:
        SB = lambda n, s, dt=F32: ctx.enter_context(nc.sbuf_tensor(n, s, dt))
        block = ctx.enter_context(nc.Block())

        sems = {k: _Sem(nc, ctx, k) for k in
                ["dS", "dG", "sT", "sV", "sA", "sG", "cc", "dO"]}
        dS, dG, sT, sV, sA, sG, cc, dO = (sems[k] for k in
                ["dS", "dG", "sT", "sV", "sA", "sG", "cc", "dO"])

        def fence(*engs):
            if not engs:
                engs = ("sync", "gpsimd", "vector", "scalar", "tensor")
            snap = [(s.h, s.n) for s in sems.values() if s.n > 0]
            for en in engs:
                def _f(e, snap=snap):
                    for h, n in snap:
                        e.wait_ge(h, n)
                getattr(block, en)(_f)

        def finish():
            pass

        ident_sb = SB("ident_sb", [128, 128])
        ident_bf = SB("ident_bf", [128, 128], BF16)
        onesc_sb = SB("onesc_sb", [128, 1])
        onescb_sb = SB("onescb_sb", [128, 1], BF16)
        onesr_sb = SB("onesr_sb", [1, 128])
        selL_sb = SB("selL_sb", [8, 1]); selR_sb = SB("selR_sb", [8, 1])
        selA_sb = SB("selA_sb", [8, 1]); selB_sb = SB("selB_sb", [8, 1])
        qpow_sb = SB("qpow_sb", [128, 1]); spow_sb = SB("spow_sb", [128, 1])
        soff_sb = SB("soff_sb", [1, 1], I32)
        mixk_c = SB("mixk_c", [128, 8]); mixv_c = SB("mixv_c", [128, 8])
        mixr_c = SB("mixr_c", [128, 8])
        wdec_c = SB("wdec_c", [128, 8]); u_c = SB("u_c", [128, 8])
        ew_c = SB("ew_c", [128, 8]); w2048_c = SB("w2048_c", [128, 8])
        b1f_c = SB("b1f_c", [128, 32]); b2f_c = SB("b2f_c", [128, 8])
        bg_sb = SB("bg_sb", [8, 1])
        wg_sb = SB("wg_sb", [128, 8 * 8])

        _cvals = [EPS, C_MAGIC, -C_MAGIC / 256.0, 1.0] + [-b for b in PWL_BKS]
        constcol_sb = SB("constcol", [128, len(_cvals)])
        _cidx = {v: i for i, v in enumerate(_cvals)}
        def ccol(val):
            i = _cidx[float(val)]
            return constcol_sb[:, i:i + 1]
        def p0_cc(v):
            for val, i in _cidx.items():
                v.memset(constcol_sb[:, i:i + 1], float(val))
            sems["sV"].inc(v.engine_nop())
        block.vector(p0_cc)

        def p0_sync(s):
            for sb_, src in [(ident_sb, ident_in), (onesc_sb, onesc_in),
                             (onesr_sb, onesr_in),
                             (selL_sb, selL_in), (selR_sb, selR_in),
                             (selA_sb, selA_in), (selB_sb, selB_in),
                             (soff_sb, soff_in), (bg_sb, bg_)]:
                dS.inc(s.dma_start(out=sb_[:, :], in_=src.ap()[:, :]), 16)
            dS.inc(s.dma_start(out=qpow_sb[:, :],
                               in_=qpow_in.ap()[:, :].to_broadcast((128, 1))), 16)
            dS.inc(s.dma_start(out=spow_sb[:, :],
                               in_=spow_in.ap()[:, :].to_broadcast((128, 1))), 16)
            dS.inc(s.dma_start(
                out=wg_sb[:, :].rearrange("p (mc e) -> p mc e", e=NE),
                in_=bass.AP(wg, 0, [[NE, 128], [128 * NE, NMC], [1, NE]])), 16)
        block.sync(p0_sync)

        ctx_slab = ExitStack()
        xn_slab = ctx_slab.enter_context(
            nc.sbuf_tensor("xn_slab", [128, NMC * SLAB], BF16))
        gate_sb = ctx_slab.enter_context(nc.sbuf_tensor("gate_sb", [8, SLAB], F32))
        ctx_ab = ExitStack()
        SBA = lambda n, s, dt=F32: ctx_ab.enter_context(nc.sbuf_tensor(n, s, dt))
        wqT_sb = SBA("wqT_sb", [128, 8 * 256], BF16)
        wkT_sb = SBA("wkT_sb", [128, 8 * 256], BF16)
        wvT_sb = SBA("wvT_sb", [128, 8 * 256], BF16)
        q_sb = [SBA(f"q_sb{p}", [128, T], BF16) for p in range(2)]
        k_sb = [SBA(f"k_sb{p}", [128, T], BF16) for p in range(2)]
        vp_sb = [SBA(f"vp_sb{h}", [128, 65 * 64], BF16) for h in range(NH)]
        qn4_sb = SBA("qn4_sb", [4, T], BF16)
        kn4_sb = SBA("kn4_sb", [4, T], BF16)
        nstage = SBA("nstage", [1, 512], BF16)

        with nc.psum_tensor("ps_t0", [128, 128], F32) as ps_t0, \
             nc.sbuf_tensor("t_stage", [32, 128], F32) as t_stage:
            vec_loads = [(mixk, mixk_c, 8), (mixv, mixv_c, 8), (mixr, mixr_c, 8),
                         (sdec, wdec_c, 8), (sfirst, u_c, 8),
                         (fb1, b1f_c, 32), (fb2, b2f_c, 8)]
            for i, (src, dst_c, rows) in enumerate(vec_loads):
                def _s(s, i=i, src=src, rows=rows):
                    if i > 0:
                        s.wait_ge(sV.h, sV.n)
                    dS.inc(s.dma_start(out=t_stage[0:rows, :], in_=src.ap()[:, :]), 16)
                block.sync(_s)
                def _t(t, rows=rows):
                    t.wait_ge(dS.h, dS.n)
                    if sV.n:
                        t.wait_ge(sV.h, sV.n)
                    sT.inc(t.transpose(ps_t0[:, 0:rows], t_stage[0:rows, :],
                                       ident_sb[0:rows, 0:rows]))
                block.tensor(_t)
                def _v(v, dst_c=dst_c, rows=rows):
                    v.wait_ge(sT.h, sT.n)
                    sV.inc(v.tensor_copy(dst_c[:, 0:rows], ps_t0[:, 0:rows]))
                block.vector(_v)
            def _v2(v):
                v.wait_ge(sV.h, sV.n)
                v.tensor_scalar_mul(wdec_c[:, :], wdec_c[:, :], 1.0 / T)
                v.tensor_scalar_mul(u_c[:, :], u_c[:, :], 1.0 / T)
                v.drain()
                sV.inc(v.tensor_scalar_mul(w2048_c[:, :], wdec_c[:, :], 2048.0))
            block.vector(_v2)
            def _a2(a):
                a.wait_ge(sV.h, sV.n)
                sA.inc(a.activation(ew_c[:, :], wdec_c[:, :], AF.Exp))
            block.scalar(_a2)
            def _v3(v):
                v.wait_ge(sA.h, sA.n)
                v.tensor_scalar(b1f_c[:, :], b1f_c[:, :], 256.0, C_MAGIC,
                                op0=ALU.mult, op1=ALU.add)
                sV.inc(v.tensor_scalar(b2f_c[:, :], b2f_c[:, :], 256.0, C_MAGIC,
                                       op0=ALU.mult, op1=ALU.add))
            block.vector(_v3)

            with nc.sbuf_tensor("wload0", [128, D], F32) as wload:
                for wsrc, wdst in [(wq_my, wqT_sb), (wk_my, wkT_sb),
                                   (wv_my, wvT_sb)]:
                    for half in range(2):
                        def _s(s, wsrc=wsrc, half=half):
                            s.wait_ge(sT.h, sT.n)
                            dS.inc(s.dma_start(
                                out=wload[:, :],
                                in_=wsrc.ap()[128 * half:128 * (half + 1), :]), 16)
                        block.sync(_s)
                        for mc in range(NMC):
                            def _t(t, mc=mc):
                                t.wait_ge(dS.h, dS.n)
                                t.wait_ge(sV.h, sV.n)
                                sT.inc(t.transpose(
                                    ps_t0[:, :],
                                    wload[:, 128 * mc:128 * (mc + 1)],
                                    ident_sb[:, :]))
                            block.tensor(_t)
                            def _v(v, wdst=wdst, half=half, mc=mc):
                                v.wait_ge(sT.h, sT.n)
                                sV.inc(v.tensor_copy(
                                    wdst[:, 256 * mc + 128 * half:
                                         256 * mc + 128 * (half + 1)],
                                    ps_t0[:, :]))
                            block.vector(_v)

        def p0_v(v):
            for h in range(NH):
                sV.inc(v.memset(vp_sb[h][:, 64:65 * 64:65], 1.0))
            v.wait_ge(dS.h, 32)
            v.tensor_copy(onescb_sb[:, :], onesc_sb[:, :])
            sV.inc(v.tensor_copy(ident_bf[:, :], ident_sb[:, :]))
        block.vector(p0_v)

        fence()

        def emit_ln_tile_db(bufs, xnbufs, sq_dump, tl, src_ap, cnts):
            par = tl % 2
            x_tm, stat = bufs[par]
            nxn = len(xnbufs)
            xn_tm = xnbufs[tl % nxn]
            def _s(s):
                s.wait_ge(sV.h, cnts.get(("xn", tl - 2), 0))
                dS.inc(s.dma_start(out=x_tm[:, :], in_=src_ap), 16)
                cnts[("dma", tl)] = dS.n
            block.sync(_s)
            def _a0(a):
                a.wait_ge(dS.h, cnts[("dma", tl)])
                a.wait_ge(sV.h, cnts.get(("st", tl - 1), 0))
                sA.inc(a.activation(sq_dump[:, :], x_tm[:, 0:D // 2],
                                    AF.Square))
                cnts[("sq0", tl)] = sA.n
            block.scalar(_a0)
            def _v0(v):
                v.wait_ge(dS.h, cnts[("dma", tl)])
                v.reduce_sum(stat[:, 0:1], x_tm[:, :], axis=AX.X)
                v.wait_ge(sA.h, cnts[("sq0", tl)])
                sV.inc(v.reduce_sum(stat[:, 1:2], sq_dump[:, :], axis=AX.X))
                cnts[("r0", tl)] = sV.n
            block.vector(_v0)
            def _a1(a):
                a.wait_ge(sV.h, cnts[("r0", tl)])
                sA.inc(a.activation(sq_dump[:, :], x_tm[:, D // 2:D],
                                    AF.Square))
                cnts[("sq1", tl)] = sA.n
            block.scalar(_a1)
            def _v(v):
                v.wait_ge(sA.h, cnts[("sq1", tl)])
                v.reduce_sum(stat[:, 4:5], sq_dump[:, :], axis=AX.X)
                v.drain()
                v.tensor_tensor(stat[:, 1:2], stat[:, 1:2], stat[:, 4:5],
                                op=ALU.add)
                v.drain()
                v.tensor_scalar_mul(stat[:, 2:3], stat[:, 0:1], 1.0 / D)
                v.tensor_scalar_mul(stat[:, 1:2], stat[:, 1:2], 1.0 / D)
                v.drain()
                v.tensor_tensor(stat[:, 3:4], stat[:, 2:3], stat[:, 2:3],
                                op=ALU.mult)
                v.drain()
                v.tensor_sub(stat[:, 1:2], stat[:, 1:2], stat[:, 3:4])
                sV.inc(v.engine_nop())
                cnts[("st", tl)] = sV.n
            block.vector(_v)
            def _a2(a):
                a.wait_ge(sV.h, cnts[("st", tl)])
                sA.inc(a.activation(stat[:, 3:4], stat[:, 1:2], AF.Sqrt,
                                    bias=ccol(EPS)))
                cnts[("sqrt", tl)] = sA.n
            block.scalar(_a2)
            def _v2(v):
                v.wait_ge(sA.h, cnts[("sqrt", tl)])
                v.wait_ge(sT.h, cnts.get(("tr", tl - nxn), 0))
                v.reciprocal(stat[:, 3:4], stat[:, 3:4])
                v.drain()
                sV.inc(v.tensor_scalar(xn_tm[:, :], x_tm[:, :], stat[:, 2:3],
                                       stat[:, 3:4], op0=ALU.subtract,
                                       op1=ALU.mult))
                cnts[("xn", tl)] = sV.n
            block.vector(_v2)
            return xn_tm

        with nc.sbuf_tensor("a1_x0", [128, D], F32) as x_tm0, \
             nc.sbuf_tensor("a1_xn0", [128, D], F32) as xn_tm0, \
             nc.sbuf_tensor("a1_st0", [128, 5], F32) as stat0, \
             nc.sbuf_tensor("a1_x1", [128, D], F32) as x_tm1, \
             nc.sbuf_tensor("a1_xn1", [128, D], F32) as xn_tm1, \
             nc.sbuf_tensor("a1_st1", [128, 5], F32) as stat1, \
             nc.sbuf_tensor("a1_sq", [128, D // 2], F32) as sq_dump, \
             nc.sbuf_tensor("a1_xf", [128, 128], F32) as xnf_ch, \
             nc.psum_tensor("a1_ptr", [128, 128], F32) as ps_tr, \
             nc.psum_tensor("a1_pg", [8, 512], F32) as ps_gate:
            bufs = [(x_tm0, stat0), (x_tm1, stat1)]
            xnbufs = [xn_tm0, xn_tm1]
            cnts = {}
            ptr_free = 0
            for tl in range(SLAB // 128):
                tt, t1 = tl // 4, tl % 4
                xn_tm = emit_ln_tile_db(
                    bufs, xnbufs, sq_dump, tl,
                    xslab.ap()[128 * tl:128 * (tl + 1), :], cnts)
                for mc in range(NMC):
                    def _t(t, mc=mc, tl=tl, xn_tm=xn_tm,
                           w=max(cnts[("xn", tl)], ptr_free)):
                        t.wait_ge(sV.h, w)
                        sT.inc(t.transpose(ps_tr[:, :],
                                           xn_tm[:, 128 * mc:128 * (mc + 1)],
                                           ident_sb[:, :]))
                        cnts[("tr", tl)] = sT.n
                    block.tensor(_t)
                    def _v(v, mc=mc, tl=tl):
                        v.wait_ge(sT.h, sT.n)
                        v.tensor_copy(
                            xn_slab[:, SLAB * mc + 128 * tl:
                                    SLAB * mc + 128 * (tl + 1)],
                            ps_tr[:, :])
                        sV.inc(v.tensor_copy(xnf_ch[:, :], ps_tr[:, :]))
                    block.vector(_v)
                    ptr_free = sV.n
                    def _t2(t, mc=mc, t1=t1):
                        t.wait_ge(sV.h, sV.n)
                        sT.inc(t.matmul(ps_gate[:, 128 * t1:128 * (t1 + 1)],
                                        wg_sb[:, 8 * mc:8 * (mc + 1)],
                                        xnf_ch[:, :],
                                        start=(mc == 0), stop=(mc == NMC - 1)))
                    block.tensor(_t2)
                if t1 == 3:
                    def _v(v, tt=tt):
                        v.wait_ge(sT.h, sT.n)
                        sV.inc(v.tensor_scalar(
                            gate_sb[:, 512 * tt:512 * (tt + 1)],
                            ps_gate[:, :], bg_sb[:, :], None, op0=ALU.add))
                    block.vector(_v)

        fence()
        if "xn_slab" in dbg:
            def _d(s):
                for mc in range(NMC):
                    dO.inc(s.dma_start(
                        out=dbg["xn_slab"].ap()[128 * mc:128 * (mc + 1), :],
                        in_=xn_slab[:, SLAB * mc:SLAB * (mc + 1)]), 16)
            block.sync(_d)
        if "gate" in dbg:
            def _d(s):
                dO.inc(s.dma_start(out=dbg["gate"].ap()[:, :],
                                   in_=gate_sb[:, :]), 16)
            block.sync(_d)

        if upto == "A1":
            _emit_zero_out(nc, block, sems, out_slab)
            return nc, dbg

        with nc.sbuf_tensor("a2_x0", [128, D], F32) as x_tm0, \
             nc.sbuf_tensor("a2_xn", [128, D], F32) as xn_tm0, \
             nc.sbuf_tensor("a2_st0", [128, 5], F32) as stat0, \
             nc.sbuf_tensor("a2_x1", [128, D], F32) as x_tm1, \
             nc.sbuf_tensor("a2_st1", [128, 5], F32) as stat1, \
             nc.sbuf_tensor("a2_sq", [128, D // 2], F32) as sq_dump, \
             nc.sbuf_tensor("a2_xf", [128, 8 * 512], BF16) as xn_fm, \
             nc.sbuf_tensor("a2_vf", [128, 256], F32) as vfm_sb, \
             nc.sbuf_tensor("a2_q2", [128, 512], BF16) as sq2, \
             nc.psum_tensor("a2_ptr", [128, 128], F32) as ps_tr, \
             nc.psum_tensor("a2_pq", [128, 2 * 512], F32) as ps_q, \
             nc.psum_tensor("a2_pk", [128, 2 * 512], F32) as ps_k, \
             nc.psum_tensor("a2_pv", [128, 2 * 512], F32) as ps_v, \
             nc.psum_tensor("a2_pn", [1, 512], F32) as ps_n:
            bufs = [(x_tm0, stat0), (x_tm1, stat1)]
            cnts = {}
            ptr_free = 0
            for tt in range(T // 512):
                for t1 in range(4):
                    tl = 4 * tt + t1
                    xn_tm = emit_ln_tile_db(
                        bufs, [xn_tm0], sq_dump, tl,
                        xseq.ap()[128 * tl:128 * (tl + 1), :], cnts)
                    for mc in range(NMC):
                        def _t(t, mc=mc, tl=tl, xn_tm=xn_tm,
                               w=max(cnts[("xn", tl)], ptr_free)):
                            t.wait_ge(sV.h, w)
                            sT.inc(t.transpose(
                                ps_tr[:, :],
                                xn_tm[:, 128 * mc:128 * (mc + 1)],
                                ident_sb[:, :]))
                            cnts[("tr", tl)] = sT.n
                        block.tensor(_t)
                        def _v(v, mc=mc, t1=t1):
                            v.wait_ge(sT.h, sT.n)
                            sV.inc(v.tensor_copy(
                                xn_fm[:, 512 * mc + 128 * t1:
                                      512 * mc + 128 * (t1 + 1)],
                                ps_tr[:, :]))
                        block.vector(_v)
                        ptr_free = sV.n
                def _t(t):
                    t.wait_ge(sV.h, sV.n)
                    for p in range(2):
                        for (wT, psd) in [(wqT_sb, ps_q), (wkT_sb, ps_k),
                                          (wvT_sb, ps_v)]:
                            for mc in range(NMC):
                                ins = t.matmul(
                                    psd[:, 512 * p:512 * (p + 1)],
                                    wT[:, 256 * mc + 128 * p:
                                       256 * mc + 128 * (p + 1)],
                                    xn_fm[:, 512 * mc:512 * (mc + 1)],
                                    start=(mc == 0), stop=(mc == NMC - 1))
                            sT.inc(ins)
                block.tensor(_t)
                for p in range(2):
                    def _v(v, p=p, tt=tt):
                        v.wait_ge(sT.h, sT.n)
                        v.tensor_copy(q_sb[p][:, 512 * tt:512 * (tt + 1)],
                                      ps_q[:, 512 * p:512 * (p + 1)])
                        sV.inc(v.tensor_copy(
                            k_sb[p][:, 512 * tt:512 * (tt + 1)],
                            ps_k[:, 512 * p:512 * (p + 1)]))
                    block.vector(_v)
                    for (psd, row0) in [(ps_q, 0), (ps_k, 4)]:
                        def _a(a, p=p, psd=psd):
                            a.wait_ge(sV.h, sV.n)
                            sA.inc(a.activation(sq2[:, :],
                                                psd[:, 512 * p:512 * (p + 1)],
                                                AF.Square))
                        block.scalar(_a)
                        for hh in range(2):
                            h = 2 * p + hh
                            def _t(t, hh=hh):
                                t.wait_ge(sA.h, sA.n)
                                t.wait_ge(sV.h, sV.n)
                                sT.inc(t.matmul(
                                    ps_n[:, :],
                                    onescb_sb[64 * hh:64 * (hh + 1), :],
                                    sq2[64 * hh:64 * (hh + 1), :],
                                    start=True, stop=True))
                            block.tensor(_t)
                            def _v2(v):
                                v.wait_ge(sT.h, sT.n)
                                v.wait_ge(dS.h, dS.n)
                                sV.inc(v.tensor_copy(nstage[:, :], ps_n[:, :]))
                            block.vector(_v2)
                            def _s2(s, h=h, row0=row0, tt=tt):
                                s.wait_ge(sV.h, sV.n)
                                dst = qn4_sb if row0 == 0 else kn4_sb
                                dS.inc(s.dma_start(
                                    out=dst[h:h + 1, 512 * tt:512 * (tt + 1)],
                                    in_=nstage[:, :]), 16)
                            block.sync(_s2)
                    for ph in range(2):
                        def _vv(v, p=p, ph=ph):
                            v.wait_ge(sT.h, sT.n)
                            sV.inc(v.tensor_copy(
                                vfm_sb[:, :],
                                ps_v[:, 512 * p + 256 * ph:
                                     512 * p + 256 * (ph + 1)]))
                        block.vector(_vv)
                        for hh in range(2):
                            h = 2 * p + hh
                            for t1 in range(2):
                                blk = 4 * tt + 2 * ph + t1
                                def _t(t, hh=hh, t1=t1):
                                    t.wait_ge(sV.h, sV.n)
                                    sT.inc(t.transpose(
                                        ps_tr[:, 0:64],
                                        vfm_sb[64 * hh:64 * (hh + 1),
                                               128 * t1:128 * (t1 + 1)],
                                        ident_sb[64 * hh:64 * (hh + 1),
                                                 64 * hh:64 * (hh + 1)]))
                                block.tensor(_t)
                                def _v3(v, h=h, blk=blk):
                                    v.wait_ge(sT.h, sT.n)
                                    sV.inc(v.tensor_copy(
                                        vp_sb[h][:, 65 * blk:65 * blk + 64],
                                        ps_tr[:, 0:64]))
                                block.vector(_v3)
                                ptr_free = sV.n

        fence()

        with nc.sbuf_tensor("kcum4", [4, T], BF16) as kcum4:
            def _a(a):
                a.activation(qn4_sb[:, :], qn4_sb[:, :], AF.Sqrt)
                sA.inc(a.activation(kn4_sb[:, :], kn4_sb[:, :], AF.Sqrt))
            block.scalar(_a)
            def _v(v):
                v.wait_ge(sA.h, sA.n)
                v.tensor_tensor_scan(kcum4[:, :], kn4_sb[:, :], kn4_sb[:, :],
                                     initial=0.0, op0=ALU.max, op1=ALU.max)
                v.drain()
                sV.inc(v.scalar_tensor_tensor(qn4_sb[:, :], qn4_sb[:, :],
                                              -0.5, kcum4[:, :],
                                              op0=ALU.mult, op1=ALU.mult))
            block.vector(_v)
            fence()
        fence()

        if "q01" in dbg:
            def _d(s):
                dO.inc(s.dma_start(out=dbg["q01"].ap()[:, :], in_=q_sb[0][:, :]), 16)
                dO.inc(s.dma_start(out=dbg["k01"].ap()[:, :], in_=k_sb[0][:, :]), 16)
                dO.inc(s.dma_start(out=dbg["vp0"].ap()[:, :], in_=vp_sb[0][:, :]), 16)
                dO.inc(s.dma_start(out=dbg["mrow"].ap()[:, :], in_=qn4_sb[:, :]), 16)
            block.sync(_d)

        if upto == "A":
            _emit_zero_out(nc, block, sems, out_slab)
            return nc, dbg

        _emit_rest(nc, block, ctx, sems, locals(), upto, dbg)
    return nc, dbg


def _emit_zero_out(nc, block, sems, out_slab):
    dS, dO = sems["dS"], sems["dO"]
    with nc.sbuf_tensor("zero_out", [128, D], F32) as z:
        def _v(v):
            sems["sV"].inc(v.memset(z[:, :], 0.0))
        block.vector(_v)
        def _s(s):
            for sem in sems.values():
                if sem.n > 0:
                    s.wait_ge(sem.h, sem.n)
            for tl in range(SLAB // 128):
                dO.inc(s.dma_start(out=out_slab.ap()[128 * tl:128 * (tl + 1), :],
                                   in_=z[:, :]), 16)
            s.wait_ge(dO.h, dO.n)
        block.sync(_s)


def _emit_rest(nc, block, ctx, sems, env, upto, dbg):
    dS, dG, sT, sV, sA, sG, cc, dO = (sems[k] for k in
            ["dS", "dG", "sT", "sV", "sA", "sG", "cc", "dO"])
    fence = env["fence"]
    ccol = env["ccol"]
    g = lambda n: env[n]
    q_sb, k_sb, vp_sb = g("q_sb"), g("k_sb"), g("vp_sb")
    qn4_sb = g("qn4_sb")
    onesr_sb, onesc_sb = g("onesr_sb"), g("onesc_sb")
    ident_sb, ident_bf = g("ident_sb"), g("ident_bf")
    masks_in, cc1_in, cc1_out = g("masks_in"), g("cc1_in"), g("cc1_out")
    groups = g("groups")

    with nc.sbuf_tensor("b_masks", [128, 4 * 512], BF16) as masks_sb, \
         nc.sbuf_tensor("b_qst", [65, 1024], BF16) as qstage, \
         nc.sbuf_tensor("b_kst", [65, 2 * 128], BF16) as kstage, \
         nc.sbuf_tensor("b_p", [128, 2 * 1024], BF16) as p_sb, \
         nc.sbuf_tensor("b_l", [1, 512], F32) as lrow, \
         nc.sbuf_tensor("b_linvb", [64, 512], F32) as linvb, \
         nc.sbuf_tensor("b_obf", [64, 1024], BF16) as obf, \
         nc.psum_tensor("b_ps_s", [128, 2 * 1024], F32) as ps_s, \
         nc.psum_tensor("b_ps_o0", [128, 512], F32) as ps_o0, \
         nc.psum_tensor("b_ps_o1", [128, 512], F32) as ps_o1:
        ps_o = [ps_o0, ps_o1]

        def _init(v):
            v.wait_ge(dS.h, dS.n)
            sV.inc(v.memset(kstage[64:65, :], 1.0))
        def _inits(s):
            dS.inc(s.dma_start(out=masks_sb[:, :], in_=masks_in.ap()[:, :]), 16)
        block.sync(_inits)
        block.vector(_init)

        for h in range(NH):
            p, hh = h // 2, h % 2
            for qc8 in range(T // 1024):
                q0 = 1024 * qc8
                nkb = 8 * (qc8 + 1)
                def _vq(v, p=p, hh=hh, q0=q0):
                    v.wait_ge(sT.h, sT.n)
                    v.wait_ge(dS.h, dS.n)
                    sV.inc(v.tensor_copy(qstage[0:64, :],
                                         q_sb[p][64 * hh:64 * (hh + 1),
                                                 q0:q0 + 1024]))
                block.vector(_vq)
                def _sq(s, h=h, q0=q0):
                    s.wait_ge(sV.h, sV.n)
                    dS.inc(s.dma_start(out=qstage[64:65, :],
                                       in_=qn4_sb[h:h + 1, q0:q0 + 1024]), 16)
                block.sync(_sq)
                exp_cnt = {}
                pv_cnt = {}
                mask_cnt = {}
                kcopy_cnt = {}
                qready = dS.n

                def emit_kcopy(kb, h=h, qc8=qc8):
                    par = kb % 2
                    def _vk(v, kb=kb, par=par, p=p, hh=hh):
                        if kb >= 2:
                            v.wait_ge(sT.h, pv_cnt.get(kb - 2, sT.n))
                        sV.inc(v.tensor_copy(
                            kstage[0:64, 128 * par:128 * (par + 1)],
                            k_sb[p][64 * hh:64 * (hh + 1),
                                    128 * kb:128 * (kb + 1)]))
                        kcopy_cnt[kb] = sV.n
                    block.vector(_vk)

                def emit_scores(kb, h=h, qc8=qc8, q0=q0):
                    par = kb % 2
                    j = kb - 8 * qc8
                    s0 = 0 if j < 0 else j // 4
                    def _t(t, kb=kb, par=par, s0=s0, q0=q0):
                        t.wait_ge(sV.h, kcopy_cnt[kb])
                        t.wait_ge(dS.h, qready)
                        if kb >= 2:
                            t.wait_ge(sA.h, exp_cnt[kb - 2])
                        for qs in range(s0, 2):
                            ins = t.matmul(
                                ps_s[:, 1024 * par + 512 * qs:
                                     1024 * par + 512 * (qs + 1)],
                                kstage[:, 128 * par:128 * (par + 1)],
                                qstage[:, 512 * qs:512 * (qs + 1)],
                                start=True, stop=True)
                        sT.inc(ins)
                        mask_cnt[("scores", kb)] = sT.n
                    block.tensor(_t)
                    def _a(a, kb=kb, par=par, s0=s0):
                        a.wait_ge(sT.h, mask_cnt[("scores", kb)])
                        if kb >= 2:
                            a.wait_ge(sT.h, pv_cnt.get(kb - 2, 0))
                        sA.inc(a.activation(
                            p_sb[:, 1024 * par + 512 * s0:1024 * (par + 1)],
                            ps_s[:, 1024 * par + 512 * s0:1024 * (par + 1)],
                            AF.Exp, scale=SCALE))
                        exp_cnt[kb] = sA.n
                    block.scalar(_a)
                    if 0 <= j < 8:
                        jm, qs_d = j % 4, j // 4
                        def _vm(v, kb=kb, par=par, jm=jm, qs_d=qs_d):
                            v.wait_ge(sA.h, exp_cnt[kb])
                            sV.inc(v.tensor_mul(
                                p_sb[:, 1024 * par + 512 * qs_d:
                                     1024 * par + 512 * (qs_d + 1)],
                                p_sb[:, 1024 * par + 512 * qs_d:
                                     1024 * par + 512 * (qs_d + 1)],
                                masks_sb[:, 512 * jm:512 * (jm + 1)]))
                            mask_cnt[kb] = sV.n
                        block.vector(_vm)

                def emit_pv(kb, h=h, qc8=qc8):
                    par = kb % 2
                    j = kb - 8 * qc8
                    s0 = 0 if j < 0 else j // 4
                    def _t(t, kb=kb, par=par, s0=s0, j=j):
                        t.wait_ge(sA.h, exp_cnt[kb])
                        if kb in mask_cnt:
                            t.wait_ge(sV.h, mask_cnt[kb])
                        for qs in range(s0, 2):
                            last = 8 * qc8 + 4 * qs + 3
                            ins = t.matmul(
                                ps_o[qs][0:65, :],
                                vp_sb[h][:, 65 * kb:65 * (kb + 1)],
                                p_sb[:, 1024 * par + 512 * qs:
                                     1024 * par + 512 * (qs + 1)],
                                start=(kb == 0), stop=(kb == last))
                        sT.inc(ins)
                        pv_cnt[kb] = sT.n
                    block.tensor(_t)

                emit_kcopy(0)
                if nkb > 1:
                    emit_kcopy(1)
                for kb in range(nkb):
                    emit_scores(kb)
                    if kb + 2 < nkb:
                        emit_kcopy(kb + 2)
                    if kb >= 1:
                        emit_pv(kb - 1)
                emit_pv(nkb - 1)

                for qs in range(2):
                    def _v(v, qs=qs):
                        v.wait_ge(sT.h, pv_cnt[nkb - 1])
                        v.tensor_scalar_max(lrow[:, :], ps_o[qs][64:65, :], 1e-30)
                        v.drain()
                        v.reciprocal(lrow[:, :], lrow[:, :])
                        sV.inc(v.engine_nop())
                    block.vector(_v)
                    def _t(t, qs=qs):
                        t.wait_ge(sV.h, sV.n)
                        sT.inc(t.matmul(ps_s[0:64, 512 * qs:512 * (qs + 1)],
                                        onesr_sb[:, 0:64], lrow[:, :],
                                        start=True, stop=True))
                    block.tensor(_t)
                    def _v2(v, qs=qs):
                        v.wait_ge(sT.h, sT.n)
                        v.wait_ge(dS.h, dS.n)
                        v.tensor_copy(linvb[:, :],
                                      ps_s[0:64, 512 * qs:512 * (qs + 1)])
                        sV.inc(v.tensor_mul(obf[:, 512 * qs:512 * (qs + 1)],
                                            ps_o[qs][0:64, :], linvb[:, :]))
                    block.vector(_v2)
                def _s(s, h=h, q0=q0):
                    s.wait_ge(sV.h, sV.n)
                    dS.inc(s.dma_start(
                        out=cc1_in.ap()[64 * h:64 * (h + 1), q0:q0 + 1024],
                        in_=obf[:, :]), 16)
                block.sync(_s)

        def _g(gp):
            gp.wait_ge(dS.h, dS.n)
            cc.inc(gp.collective_compute(
                "AllGather", ALU.bypass, replica_groups=groups,
                ins=[cc1_in.ap()[:, :]], outs=[cc1_out.ap()[:, :]]), 1)
        block.gpsimd(_g)

    fence()
    if "osl" in dbg:
        with nc.sbuf_tensor("dbg_osl_sb", [128, SLAB], BF16) as osb:
            soff_sb = g("soff_sb")
            for mc in range(NMC):
                def _gp(gp, mc=mc):
                    gp.wait_ge(cc.h, cc.n)
                    if mc > 0:
                        gp.wait_ge(dO.h, dO.n)
                    with gp.register(f"ro{mc}") as ro:
                        gp.reg_load(ro, soff_sb[0:1, 0:1])
                        gp.reg_add(ro, ro, 128 * mc * T)
                        dG.inc(gp.dma_start(
                            out=osb[:, :],
                            in_=bass.AP(cc1_out, ro, [[T, 128], [1, SLAB]])), 16)
                    gp.wait_ge(dG.h, dG.n)
                    dO.inc(gp.dma_start(
                        out=dbg["osl"].ap()[128 * mc:128 * (mc + 1), :],
                        in_=osb[:, :]), 16)
                block.gpsimd(_gp)
        fence()

    if upto == "B":
        env["ctx_ab"].close()
        _emit_zero_out(nc, block, sems, env["out_slab"])
        return

    env["ctx_ab"].close()
    gate_sb, xn_slab = g("gate_sb"), g("xn_slab")
    b2moe, b1moe = g("b2moe"), g("b1moe")
    w1moe, w2moe = g("w1moe"), g("w2moe")

    ctx_cd = ExitStack()
    acc = ctx_cd.enter_context(nc.sbuf_tensor("acc", [128, NMC * SLAB], F32))
    esel_sb = ctx_cd.enter_context(nc.sbuf_tensor("c_esel", [8, 8 * 128], F32))

    ctx_c = ExitStack()
    SBC = lambda n, s, dt=F32: ctx_c.enter_context(nc.sbuf_tensor(n, s, dt))
    if True:
        b2moe_sb = SBC("c_b2", [8, D]); wfm = SBC("c_wfm", [8, SLAB])
        gt = SBC("c_gt", [128, 16]); gstat = SBC("c_gs", [128, 8])
        wb_sb = SBC("c_wb", [128, SLAB], BF16)
        b1stage = SBC("c_b1s", [32, 128]); b1T = SBC("c_b1T", [128, 32])
        w1f = [SBC(f"c_w1f{i}", [128, 1024]) for i in range(2)]
        w1b = [SBC(f"c_w1b{i}", [128, 1024], BF16) for i in range(2)]
        w2f = [SBC(f"c_w2f{i}", [128, 1024]) for i in range(2)]
        w2b = [SBC(f"c_w2b{i}", [128, 1024], BF16) for i in range(2)]
        h_sb = SBC("c_h", [128, 8 * SLAB], BF16)
        ps_big = ctx_c.enter_context(nc.psum_tensor("c_big", [128, 2048], F32))
        ps_sml = ctx_c.enter_context(nc.psum_tensor("c_sml", [128, 512], F32))
        ident_sb = g("ident_sb")
        bg_sb = g("bg_sb")

        def _s0(s):
            dS.inc(s.dma_start(out=b2moe_sb[:, :], in_=b2moe.ap()[:, :]), 16)
            dS.inc(s.dma_start(out=esel_sb[:, :], in_=env["din"]["esel"].ap()[:, :]), 16)
        block.sync(_s0)

        for gtl in range(SLAB // 128):
            def _t(t, gtl=gtl):
                t.wait_ge(sV.h, sV.n)
                sT.inc(t.transpose(ps_sml[:, 0:8],
                                   gate_sb[:, 128 * gtl:128 * (gtl + 1)],
                                   ident_sb[0:8, 0:8]))
            block.tensor(_t)
            def _v(v):
                v.wait_ge(sT.h, sT.n)
                v.tensor_copy(gt[:, 0:8], ps_sml[:, 0:8])
                v.drain()
                v.reduce_max(gstat[:, 0:1], gt[:, 0:8], axis=AX.X)
                v.drain()
                sV.inc(v.tensor_scalar_mul(gstat[:, 1:2], gstat[:, 0:1], -1.0))
            block.vector(_v)
            def _a(a):
                a.wait_ge(sV.h, sV.n)
                sA.inc(a.activation(gt[:, 8:16], gt[:, 0:8], AF.Exp,
                                    bias=gstat[:, 1:2]))
            block.scalar(_a)
            def _v2(v, gtl=gtl):
                v.wait_ge(sA.h, sA.n)
                v.reduce_sum(gstat[:, 2:3], gt[:, 8:16], axis=AX.X)
                v.drain()
                v.reciprocal(gstat[:, 2:3], gstat[:, 2:3])
                v.drain()
                v.tensor_scalar_mul(gt[:, 0:8], gt[:, 8:16], gstat[:, 2:3])
                v.drain()
                v.reduce_max(gstat[:, 3:4], gt[:, 0:8], axis=AX.X)
                v.drain()
                v.tensor_scalar(gt[:, 8:16], gt[:, 0:8], gstat[:, 3:4], None,
                                op0=ALU.is_equal)
                v.drain()
                v.scalar_tensor_tensor(gt[:, 8:16], gt[:, 8:16], -1e30,
                                       gt[:, 0:8], op0=ALU.mult, op1=ALU.add)
                v.drain()
                v.reduce_max(gstat[:, 4:5], gt[:, 8:16], axis=AX.X)
                v.drain()
                v.tensor_scalar(gt[:, 8:16], gt[:, 0:8], gstat[:, 4:5], None,
                                op0=ALU.is_ge)
                v.drain()
                sV.inc(v.tensor_mul(gt[:, 8:16], gt[:, 0:8], gt[:, 8:16]))
            block.vector(_v2)
            def _t2(t, gtl=gtl):
                t.wait_ge(sV.h, sV.n)
                sT.inc(t.transpose(ps_sml[0:8, 128:256], gt[:, 8:16],
                                   ident_sb[:, :]))
            block.tensor(_t2)
            def _v3(v, gtl=gtl):
                v.wait_ge(sT.h, sT.n)
                sV.inc(v.tensor_copy(wfm[:, 128 * gtl:128 * (gtl + 1)],
                                     ps_sml[0:8, 128:256]))
            block.vector(_v3)

        for dc in range(NMC):
            for tt in range(4):
                def _t(t, dc=dc, tt=tt):
                    t.wait_ge(sV.h, sV.n)
                    t.wait_ge(dS.h, dS.n)
                    sT.inc(t.matmul(ps_sml[:, :],
                                    b2moe_sb[:, 128 * dc:128 * (dc + 1)],
                                    wfm[:, 512 * tt:512 * (tt + 1)],
                                    start=True, stop=True))
                block.tensor(_t)
                def _v(v, dc=dc, tt=tt):
                    v.wait_ge(sT.h, sT.n)
                    sV.inc(v.tensor_copy(
                        acc[:, SLAB * dc + 512 * tt:SLAB * dc + 512 * (tt + 1)],
                        ps_sml[:, :]))
                block.vector(_v)

        gelu_cnt = {}
        mm_cnt = {}
        conv_cnt = {}
        scale_cnt = {}
        evac_cnt = {}
        for e in range(NE):
            for tt in range(4):
                def _t(t, e=e, tt=tt):
                    t.wait_ge(sV.h, sV.n)
                    sT.inc(t.matmul(ps_sml[:, :],
                                    esel_sb[:, 128 * e:128 * (e + 1)],
                                    wfm[:, 512 * tt:512 * (tt + 1)],
                                    start=True, stop=True))
                block.tensor(_t)
                def _v(v, tt=tt):
                    v.wait_ge(sT.h, sT.n)
                    sV.inc(v.tensor_copy(wb_sb[:, 512 * tt:512 * (tt + 1)],
                                         ps_sml[:, :]))
                block.vector(_v)
            def _s(s, e=e):
                s.wait_ge(sT.h, sT.n)
                dS.inc(s.dma_start(out=b1stage[:, :], in_=b1moe.ap()[e, :, :]), 16)
            block.sync(_s)
            def _t(t):
                t.wait_ge(dS.h, dS.n)
                t.wait_ge(sV.h, sV.n)
                sT.inc(t.transpose(ps_sml[:, 0:32], b1stage[:, :],
                                   ident_sb[0:32, 0:32]))
            block.tensor(_t)
            def _v(v):
                v.wait_ge(sT.h, sT.n)
                sV.inc(v.tensor_copy(b1T[:, :], ps_sml[:, 0:32]))
            block.vector(_v)

            for qu in range(4):
                idx0 = 8 * (4 * e + qu)
                for hh in range(8):
                    gi = idx0 + hh
                    par = gi % 2
                    c0 = 1024 * qu + 128 * hh
                    def _s(s, e=e, c0=c0, par=par, gi=gi):
                        if gi >= 2:
                            s.wait_ge(sG.h, conv_cnt.get(gi - 2, 0))
                        dS.inc(s.dma_start(
                            out=w1f[par][:, :].rearrange("p (mc c) -> p mc c", c=128),
                            in_=bass.AP(w1moe, e * D * FF + c0,
                                        [[FF, 128], [128 * FF, NMC], [1, 128]])), 16)
                        conv_cnt[("dma1", gi)] = dS.n
                    block.sync(_s)
                    def _g2(gp, gi=gi, par=par):
                        gp.wait_ge(dS.h, conv_cnt[("dma1", gi)])
                        if gi >= 2:
                            gp.wait_ge(sT.h, mm_cnt.get(gi - 2, 0))
                        sG.inc(gp.tensor_copy(w1b[par][:, :], w1f[par][:, :]))
                        conv_cnt[gi] = sG.n
                    block.gpsimd(_g2)
                    def _t(t, gi=gi, par=par, hh=hh):
                        t.wait_ge(sG.h, conv_cnt[gi])
                        if hh == 0:
                            t.wait_ge(sV.h, sV.n)
                        for half in range(2):
                            if gi >= 1:
                                for tt in (2 * half, 2 * half + 1):
                                    t.wait_ge(sA.h, gelu_cnt.get((gi - 1, tt), 0))
                            for mc in range(NMC):
                                for tt in (2 * half, 2 * half + 1):
                                    ins = t.matmul(
                                        ps_big[:, 512 * tt:512 * (tt + 1)],
                                        w1b[par][:, 128 * mc:128 * (mc + 1)],
                                        xn_slab[:, SLAB * mc + 512 * tt:
                                                SLAB * mc + 512 * (tt + 1)],
                                        start=(mc == 0), stop=(mc == NMC - 1))
                                    if mc == NMC - 1:
                                        sT.inc(ins)
                                        mm_cnt[(gi, tt)] = sT.n
                        mm_cnt[gi] = sT.n
                    block.tensor(_t)
                    for tt in range(4):
                        def _a(a, gi=gi, tt=tt, hh=hh, qu=qu):
                            a.wait_ge(sT.h, mm_cnt[(gi, tt)])
                            a.wait_ge(sV.h, sV.n)
                            if hh == 0:
                                a.wait_ge(sG.h, sG.n)
                            sA.inc(a.activation(
                                h_sb[:, SLAB * hh + 512 * tt:
                                     SLAB * hh + 512 * (tt + 1)],
                                ps_big[:, 512 * tt:512 * (tt + 1)],
                                AF.Gelu, bias=b1T[:, 8 * qu + hh:8 * qu + hh + 1]))
                            gelu_cnt[(gi, tt)] = sA.n
                        block.scalar(_a)
                def _g3(v, idx0=idx0):
                    for hh in range(8):
                        v.wait_ge(sA.h, gelu_cnt[(idx0 + hh, 3)])
                        sV.inc(v.tensor_mul(h_sb[:, SLAB * hh:SLAB * (hh + 1)],
                                            h_sb[:, SLAB * hh:SLAB * (hh + 1)],
                                            wb_sb[:, :]))
                        scale_cnt[idx0 + hh] = sV.n
                block.vector(_g3)

                for dc in range(NMC):
                    gi = idx0 + dc
                    par = dc % 2
                    def _s(s, e=e, qu=qu, dc=dc, par=par, gi=gi):
                        if dc >= 2:
                            s.wait_ge(sG.h, conv_cnt.get(("c2", gi - 2), 0))
                        dS.inc(s.dma_start(
                            out=w2f[par][:, :].rearrange("p (i c) -> p i c", c=128),
                            in_=bass.AP(w2moe,
                                        e * FF * D + 1024 * qu * D + 128 * dc,
                                        [[D, 128], [128 * D, 8], [1, 128]])), 16)
                        conv_cnt[("dma2", gi)] = dS.n
                    block.sync(_s)
                    def _g2(gp, gi=gi, par=par, dc=dc, idx0=idx0):
                        gp.wait_ge(dS.h, conv_cnt[("dma2", gi)])
                        if dc == 0:
                            gp.wait_ge(sV.h, scale_cnt[idx0 + 7])
                        if dc >= 2:
                            gp.wait_ge(sT.h, mm_cnt.get(("m2", gi - 2), 0))
                        sG.inc(gp.tensor_copy(w2b[par][:, :], w2f[par][:, :]))
                        conv_cnt[("c2", gi)] = sG.n
                    block.gpsimd(_g2)
                    def _t(t, gi=gi, par=par, dc=dc, idx0=idx0):
                        t.wait_ge(sG.h, conv_cnt[("c2", gi)])
                        t.wait_ge(sV.h, scale_cnt[idx0 + 7])
                        for half in range(2):
                            if dc >= 1:
                                t.wait_ge(sV.h,
                                          evac_cnt.get((gi - 1, half), 0))
                            for i in range(8):
                                for tt in (2 * half, 2 * half + 1):
                                    ins = t.matmul(
                                        ps_big[:, 512 * tt:512 * (tt + 1)],
                                        w2b[par][:, 128 * i:128 * (i + 1)],
                                        h_sb[:, SLAB * i + 512 * tt:
                                             SLAB * i + 512 * (tt + 1)],
                                        start=(i == 0), stop=(i == 7))
                                    if i == 7:
                                        sT.inc(ins)
                            mm_cnt[("m2", gi, half)] = sT.n
                        mm_cnt[("m2", gi)] = sT.n
                    block.tensor(_t)
                    def _v(v, gi=gi, dc=dc):
                        for half in range(2):
                            v.wait_ge(sT.h, mm_cnt[("m2", gi, half)])
                            for tt in (2 * half, 2 * half + 1):
                                v.tensor_add(
                                    acc[:, SLAB * dc + 512 * tt:
                                        SLAB * dc + 512 * (tt + 1)],
                                    acc[:, SLAB * dc + 512 * tt:
                                        SLAB * dc + 512 * (tt + 1)],
                                    ps_big[:, 512 * tt:512 * (tt + 1)])
                            sV.inc(v.engine_nop())
                            evac_cnt[(gi, half)] = sV.n
                        evac_cnt[gi] = sV.n
                    block.vector(_v)

    fence()
    if "wfm" in dbg:
        def _d(s):
            dO.inc(s.dma_start(out=dbg["wfm"].ap()[:, :], in_=wfm[:, :]), 16)
        block.sync(_d)
    ctx_c.close()
    fence()
    if "x1" in dbg:
        def _d(s):
            for mc in range(NMC):
                dO.inc(s.dma_start(
                    out=dbg["x1"].ap()[128 * mc:128 * (mc + 1), :],
                    in_=acc[:, SLAB * mc:SLAB * (mc + 1)]), 16)
        block.sync(_d)

    if upto == "C":
        _emit_zero_out(nc, block, sems, env["out_slab"])
        ctx_cd.close()
        return

    wo0, xslab = g("wo0"), g("xslab")
    cc3_in, cc3_out, x1spill = g("cc3_in"), g("cc3_out"), g("x1spill")
    soff_sb = g("soff_sb")
    ident_sb = g("ident_sb")

    ctx_d = ExitStack()
    SBD = lambda n, s, dt=F32: ctx_d.enter_context(nc.sbuf_tensor(n, s, dt))
    woT_sb = SBD("d_woT", [128, 8 * 1024], BF16)
    wload = SBD("d_wl", [128, 1024])
    osl_sb = SBD("d_osl", [128, NMC * SLAB], BF16)
    xd_tm = SBD("d_x", [128, 1024])
    ps_d = ctx_d.enter_context(nc.psum_tensor("d_ps", [128, 512], F32))
    ps_tr = ctx_d.enter_context(nc.psum_tensor("d_ptr", [128, 128], F32))

    for nck in range(NMC):
        def _s(s, nck=nck):
            s.wait_ge(sT.h, sT.n)
            dS.inc(s.dma_start(out=wload[:, :],
                               in_=wo0.ap()[128 * nck:128 * (nck + 1), :]), 16)
        block.sync(_s)
        for mc in range(NMC):
            def _t(t, mc=mc):
                t.wait_ge(dS.h, dS.n)
                t.wait_ge(sV.h, sV.n)
                sT.inc(t.transpose(ps_tr[:, :],
                                   wload[:, 128 * mc:128 * (mc + 1)],
                                   ident_sb[:, :]))
            block.tensor(_t)
            def _v(v, mc=mc, nck=nck):
                v.wait_ge(sT.h, sT.n)
                sV.inc(v.tensor_copy(
                    woT_sb[:, 1024 * mc + 128 * nck:1024 * mc + 128 * (nck + 1)],
                    ps_tr[:, :]))
            block.vector(_v)

    for mc in range(NMC):
        def _gp(gp, mc=mc):
            gp.wait_ge(cc.h, 1)
            with gp.register(f"do{mc}") as ro:
                gp.reg_load(ro, soff_sb[0:1, 0:1])
                gp.reg_add(ro, ro, 128 * mc * T)
                dG.inc(gp.dma_start(
                    out=osl_sb[:, SLAB * mc:SLAB * (mc + 1)],
                    in_=bass.AP(g("cc1_out"), ro, [[T, 128], [1, SLAB]])), 16)
        block.gpsimd(_gp)

    for dc in range(NMC):
        for tt in range(4):
            def _t(t, dc=dc, tt=tt):
                t.wait_ge(dG.h, dG.n)
                t.wait_ge(sV.h, sV.n)
                for mc in range(NMC):
                    ins = t.matmul(
                        ps_d[:, :],
                        woT_sb[:, 1024 * mc + 128 * dc:1024 * mc + 128 * (dc + 1)],
                        osl_sb[:, SLAB * mc + 512 * tt:SLAB * mc + 512 * (tt + 1)],
                        start=(mc == 0), stop=(mc == NMC - 1))
                sT.inc(ins)
            block.tensor(_t)
            def _v(v, dc=dc, tt=tt):
                v.wait_ge(sT.h, sT.n)
                sV.inc(v.tensor_add(
                    acc[:, SLAB * dc + 512 * tt:SLAB * dc + 512 * (tt + 1)],
                    acc[:, SLAB * dc + 512 * tt:SLAB * dc + 512 * (tt + 1)],
                    ps_d[:, :]))
            block.vector(_v)

    for tl in range(SLAB // 128):
        def _s(s, tl=tl):
            s.wait_ge(sT.h, sT.n)
            dS.inc(s.dma_start(out=xd_tm[:, :],
                               in_=xslab.ap()[128 * tl:128 * (tl + 1), :]), 16)
        block.sync(_s)
        for mc in range(NMC):
            def _t(t, mc=mc):
                t.wait_ge(dS.h, dS.n)
                t.wait_ge(sV.h, sV.n)
                sT.inc(t.transpose(ps_tr[:, :],
                                   xd_tm[:, 128 * mc:128 * (mc + 1)],
                                   ident_sb[:, :]))
            block.tensor(_t)
            def _v(v, mc=mc, tl=tl):
                v.wait_ge(sT.h, sT.n)
                sV.inc(v.tensor_add(
                    acc[:, SLAB * mc + 128 * tl:SLAB * mc + 128 * (tl + 1)],
                    acc[:, SLAB * mc + 128 * tl:SLAB * mc + 128 * (tl + 1)],
                    ps_tr[:, :]))
            block.vector(_v)

    def _s(s):
        s.wait_ge(sV.h, sV.n)
        for mc in range(NMC):
            dS.inc(s.dma_start(out=x1spill.ap()[128 * mc:128 * (mc + 1), :],
                               in_=acc[:, SLAB * mc:SLAB * (mc + 1)]), 16)
        with nc.allow_non_contiguous_dma(reason="tiny halo rows"):
            for mc in range(NMC):
                dS.inc(s.dma_start(
                    out=cc3_in.ap()[0:1, 128 * mc:128 * (mc + 1)],
                    in_=acc[:, SLAB * mc:SLAB * mc + 1],
                    allow_slow_non_contiguous=True), 16)
                dS.inc(s.dma_start(
                    out=cc3_in.ap()[1:2, 128 * mc:128 * (mc + 1)],
                    in_=acc[:, SLAB * mc + SLAB - 1:SLAB * mc + SLAB],
                    allow_slow_non_contiguous=True), 16)
    block.sync(_s)
    def _gp(gp):
        gp.wait_ge(dS.h, dS.n)
        cc.inc(gp.collective_compute(
            "AllGather", ALU.bypass, replica_groups=groups,
            ins=[cc3_in.ap()[:, :]], outs=[cc3_out.ap()[:, :]]), 1)
    block.gpsimd(_gp)

    ctx_d.close()
    fence()
    if "x1" in dbg:
        def _d(s):
            for mc in range(NMC):
                dO.inc(s.dma_start(
                    out=dbg["x1"].ap()[128 * mc:128 * (mc + 1), :],
                    in_=acc[:, SLAB * mc:SLAB * (mc + 1)]), 16)
        block.sync(_d)

    if upto == "D":
        _emit_zero_out(nc, block, sems, env["out_slab"])
        ctx_cd.close()
        return

    ctx_cd.close()
    env["ctx_slab"].close()
    fence()
    l1wk, l1wv, l1wr, l1wo = g("l1wk"), g("l1wv"), g("l1wr"), g("l1wo")
    fw1, fw2 = g("fw1"), g("fw2")
    ccs_in, ccs_out = g("ccs_in"), g("ccs_out")
    mixk_c, mixv_c, mixr_c = g("mixk_c"), g("mixv_c"), g("mixr_c")
    wdec_c, u_c, ew_c, w2048_c = g("wdec_c"), g("u_c"), g("ew_c"), g("w2048_c")
    b1f_c, b2f_c = g("b1f_c"), g("b2f_c")
    qpow_sb, spow_sb = g("qpow_sb"), g("spow_sb")
    selL_sb, selR_sb = g("selL_sb"), g("selR_sb")
    selA_sb, selB_sb = g("selA_sb"), g("selB_sb")
    onesc_sb, onesr_sb = g("onesc_sb"), g("onesr_sb")
    out_slab = g("out_slab")

    ctx_e = ExitStack()
    SBE = lambda n, s, dt=F32: ctx_e.enter_context(nc.sbuf_tensor(n, s, dt))
    wkT1 = SBE("e_wkT", [128, 8 * 1024], BF16)
    wvT1 = SBE("e_wvT", [128, 8 * 1024], BF16)
    wrT1 = SBE("e_wrT", [128, 8 * 1024], BF16)
    woT1 = SBE("e_woT", [128, 8 * 1024], BF16)
    xn1_bf = SBE("e_xn1", [128, NMC * SLAB], BF16)
    ROWB = SLAB + 2
    rows_ln = SBE("e_rows", [1, 2 * (SLAB + 2)])
    x1h = SBE("e_x1h", [128, 16])
    xn1h = SBE("e_xn1h", [128, 16])
    carry_a = SBE("e_ca", [128, 8]); carry_b = SBE("e_cb", [128, 8])
    gself = SBE("e_gs", [128, 8]); gq_c = SBE("e_gq", [128, 8])
    ewb = SBE("e_ewb", [128, 512])

    with nc.sbuf_tensor("e_wl", [128, 1024], F32) as wload, \
         nc.psum_tensor("e_ptr", [128, 128], F32) as ps_tr:
        for wsrc, wdst in [(l1wk, wkT1), (l1wv, wvT1), (l1wr, wrT1),
                           (l1wo, woT1)]:
            for nck in range(NMC):
                def _s(s, wsrc=wsrc, nck=nck):
                    s.wait_ge(sT.h, sT.n)
                    dS.inc(s.dma_start(
                        out=wload[:, :],
                        in_=wsrc.ap()[128 * nck:128 * (nck + 1), :]), 16)
                block.sync(_s)
                for mc in range(NMC):
                    def _t(t, mc=mc):
                        t.wait_ge(dS.h, dS.n)
                        t.wait_ge(sV.h, sV.n)
                        sT.inc(t.transpose(ps_tr[:, :],
                                           wload[:, 128 * mc:128 * (mc + 1)],
                                           ident_sb[:, :]))
                    block.tensor(_t)
                    def _v(v, wdst=wdst, mc=mc, nck=nck):
                        v.wait_ge(sT.h, sT.n)
                        sV.inc(v.tensor_copy(
                            wdst[:, 1024 * mc + 128 * nck:
                                 1024 * mc + 128 * (nck + 1)],
                            ps_tr[:, :]))
                    block.vector(_v)
        def _v(v):
            v.wait_ge(sV.h, sV.n)
            v.tensor_scalar(gself[:, :], w2048_c[:, :], spow_sb[:, 0:1], None,
                            op0=ALU.mult)
            sV.inc(v.tensor_scalar(gq_c[:, :], w2048_c[:, :], qpow_sb[:, 0:1],
                                   None, op0=ALU.mult))
        block.vector(_v)
        def _a(a):
            a.wait_ge(sV.h, sV.n)
            a.activation(gself[:, :], gself[:, :], AF.Exp)
            sA.inc(a.activation(gq_c[:, :], gq_c[:, :], AF.Exp))
        block.scalar(_a)

        with nc.sbuf_tensor("e_c3", [8, D], F32) as c3_sb, \
             nc.sbuf_tensor("e_hsq", [128, 8], F32) as hsq, \
             nc.sbuf_tensor("e_hst", [1, 8], F32) as hst, \
             nc.sbuf_tensor("e_hab", [128, 2], F32) as hab, \
             nc.psum_tensor("e_p1", [128, 8], F32) as ps_h1, \
             nc.psum_tensor("e_p2", [1, 2], F32) as ps_h2:
            def _s(s):
                s.wait_ge(cc.h, 2)
                dS.inc(s.dma_start(out=c3_sb[:, :], in_=cc3_out.ap()[:, :]), 16)
            block.sync(_s)
            for side, sel in [(0, selL_sb), (1, selR_sb)]:
                for mc in range(NMC):
                    def _t(t, mc=mc, side=side, sel=sel):
                        t.wait_ge(dS.h, dS.n)
                        t.wait_ge(sV.h, sV.n)
                        sT.inc(t.matmul(ps_h1[:, mc:mc + 1],
                                        c3_sb[:, 128 * mc:128 * (mc + 1)],
                                        sel[:, :], start=True, stop=True))
                    block.tensor(_t)
                def _v(v, side=side):
                    v.wait_ge(sT.h, sT.n)
                    sV.inc(v.tensor_copy(x1h[:, 8 * side:8 * (side + 1)],
                                         ps_h1[:, :]))
                block.vector(_v)
            for side in range(2):
                def _a(a, side=side):
                    a.wait_ge(sV.h, sV.n)
                    sA.inc(a.activation(hsq[:, :],
                                        x1h[:, 8 * side:8 * (side + 1)],
                                        AF.Square))
                block.scalar(_a)
                def _t(t, side=side):
                    t.wait_ge(sA.h, sA.n)
                    if side == 1:
                        t.wait_ge(sV.h, sV.n)
                    for mc in range(NMC):
                        t.matmul(ps_h2[:, 0:1],
                                 x1h[:, 8 * side + mc:8 * side + mc + 1],
                                 onesc_sb[:, :],
                                 start=(mc == 0), stop=(mc == NMC - 1))
                    for mc in range(NMC):
                        ins = t.matmul(ps_h2[:, 1:2], hsq[:, mc:mc + 1],
                                       onesc_sb[:, :],
                                       start=(mc == 0), stop=(mc == NMC - 1))
                    sT.inc(ins)
                block.tensor(_t)
                def _v(v, side=side):
                    v.wait_ge(sT.h, sT.n)
                    v.tensor_scalar_mul(hst[:, 0:1], ps_h2[:, 0:1], 1.0 / D)
                    v.tensor_scalar_mul(hst[:, 1:2], ps_h2[:, 1:2], 1.0 / D)
                    v.drain()
                    v.tensor_mul(hst[:, 2:3], hst[:, 0:1], hst[:, 0:1])
                    v.drain()
                    v.tensor_sub(hst[:, 3:4], hst[:, 1:2], hst[:, 2:3])
                    sV.inc(v.engine_nop())
                block.vector(_v)
                def _a2(a):
                    a.wait_ge(sV.h, sV.n)
                    sA.inc(a.activation(hst[:, 4:5], hst[:, 3:4], AF.Sqrt,
                                        bias=ccol(EPS)[0:1, :]))
                block.scalar(_a2)
                def _v2(v, side=side):
                    v.wait_ge(sA.h, sA.n)
                    v.reciprocal(hst[:, 4:5], hst[:, 4:5])
                    v.drain()
                    v.scalar_tensor_tensor(hst[:, 5:6], hst[:, 0:1], -1.0,
                                           hst[:, 4:5], op0=ALU.mult,
                                           op1=ALU.mult)
                    v.drain()
                    col = 0 if side == 0 else SLAB + 1
                    v.tensor_copy(rows_ln[0:1, col:col + 1], hst[:, 4:5])
                    sV.inc(v.tensor_copy(rows_ln[0:1, ROWB + col:ROWB + col + 1],
                                         hst[:, 5:6]))
                block.vector(_v2)
                def _t2(t, side=side):
                    t.wait_ge(sV.h, sV.n)
                    t.matmul(ps_h1[:, 0:1], onesr_sb[:, :], hst[:, 4:5],
                             start=True, stop=True)
                    sT.inc(t.matmul(ps_h1[:, 1:2], onesr_sb[:, :], hst[:, 5:6],
                                    start=True, stop=True))
                block.tensor(_t2)
                def _v3(v, side=side):
                    v.wait_ge(sT.h, sT.n)
                    v.tensor_copy(hab[:, :], ps_h1[:, 0:2])
                    v.drain()
                    sV.inc(v.tensor_scalar(xn1h[:, 8 * side:8 * (side + 1)],
                                           x1h[:, 8 * side:8 * (side + 1)],
                                           hab[:, 0:1], hab[:, 1:2],
                                           op0=ALU.mult, op1=ALU.add))
                block.vector(_v3)

    fence()

    omk_c = ctx_e.enter_context(nc.sbuf_tensor("e_omk", [128, 8], F32))
    omv_c = ctx_e.enter_context(nc.sbuf_tensor("e_omv", [128, 8], F32))
    omr_c = ctx_e.enter_context(nc.sbuf_tensor("e_omr", [128, 8], F32))
    def _v0(v):
        v.tensor_scalar(omk_c[:, :], mixk_c[:, :], -1.0, 1.0,
                        op0=ALU.mult, op1=ALU.add)
        v.tensor_scalar(omv_c[:, :], mixv_c[:, :], -1.0, 1.0,
                        op0=ALU.mult, op1=ALU.add)
        v.tensor_scalar(omr_c[:, :], mixr_c[:, :], -1.0, 1.0,
                        op0=ALU.mult, op1=ALU.add)
        v.memset(carry_a[:, :], 0.0)
        sV.inc(v.memset(carry_b[:, :], 0.0))
    block.vector(_v0)

    with nc.sbuf_tensor("ea_x1", [128, 8 * 512], F32) as x1_t, \
         nc.sbuf_tensor("ea_sq", [128, 512], F32) as sq_t, \
         nc.sbuf_tensor("ea_rw", [1, 2048], F32) as rw, \
         nc.sbuf_tensor("ea_t1", [128, 512], F32) as t1_sb, \
         nc.psum_tensor("ea_p1", [1, 512], F32) as ps_r1, \
         nc.psum_tensor("ea_p2", [1, 512], F32) as ps_r2, \
         nc.psum_tensor("ea_bc", [128, 1024], F32) as ps_bc:
        for tt in range(4):
            mmc = {}
            def _s(s, tt=tt):
                s.wait_ge(sV.h, sV.n)
                s.wait_ge(sA.h, sA.n)
                for mc in range(NMC):
                    dS.inc(s.dma_start(
                        out=x1_t[:, 512 * mc:512 * (mc + 1)],
                        in_=x1spill.ap()[128 * mc:128 * (mc + 1),
                                         512 * tt:512 * (tt + 1)]), 16)
            block.sync(_s)
            for mc in range(NMC):
                def _a(a, mc=mc):
                    a.wait_ge(dS.h, dS.n)
                    if mc == 0:
                        a.wait_ge(sT.h, sT.n)
                    else:
                        a.wait_ge(sT.h, mmc[mc - 1])
                    sA.inc(a.activation(sq_t[:, :],
                                        x1_t[:, 512 * mc:512 * (mc + 1)],
                                        AF.Square))
                block.scalar(_a)
                def _t(t, mc=mc):
                    t.wait_ge(sA.h, sA.n)
                    if mc == 0:
                        t.wait_ge(sV.h, sV.n)
                    t.matmul(ps_r1[:, :], onesc_sb[:, :],
                             x1_t[:, 512 * mc:512 * (mc + 1)],
                             start=(mc == 0), stop=(mc == NMC - 1))
                    ins = t.matmul(ps_r2[:, :], onesc_sb[:, :], sq_t[:, :],
                                   start=(mc == 0), stop=(mc == NMC - 1))
                    sT.inc(ins)
                    mmc[mc] = sT.n
                block.tensor(_t)
            def _v(v, tt=tt):
                v.wait_ge(sT.h, sT.n)
                v.tensor_scalar_mul(rw[:, 0:512], ps_r1[:, :], 1.0 / D)
                v.tensor_scalar_mul(rw[:, 512:1024], ps_r2[:, :], 1.0 / D)
                v.drain()
                v.tensor_mul(rw[:, 1024:1536], rw[:, 0:512], rw[:, 0:512])
                v.drain()
                v.tensor_sub(rw[:, 1024:1536], rw[:, 512:1024], rw[:, 1024:1536])
                sV.inc(v.engine_nop())
            block.vector(_v)
            def _a2(a):
                a.wait_ge(sV.h, sV.n)
                sA.inc(a.activation(rw[:, 1536:2048], rw[:, 1024:1536],
                                    AF.Sqrt, bias=ccol(EPS)[0:1, :]))
            block.scalar(_a2)
            def _v2(v, tt=tt):
                v.wait_ge(sA.h, sA.n)
                v.reciprocal(rw[:, 1536:2048], rw[:, 1536:2048])
                v.drain()
                v.tensor_copy(rows_ln[0:1, 1 + 512 * tt:1 + 512 * (tt + 1)],
                              rw[:, 1536:2048])
                v.scalar_tensor_tensor(rw[:, 512:1024], rw[:, 0:512], -1.0,
                                       rw[:, 1536:2048], op0=ALU.mult,
                                       op1=ALU.mult)
                v.drain()
                sV.inc(v.tensor_copy(
                    rows_ln[0:1, ROWB + 1 + 512 * tt:ROWB + 1 + 512 * (tt + 1)],
                    rw[:, 512:1024]))
            block.vector(_v2)
            def _t2(t, tt=tt):
                t.wait_ge(sV.h, sV.n)
                t.matmul(ps_bc[:, 0:512], onesr_sb[:, :], rw[:, 1536:2048],
                         start=True, stop=True)
                sT.inc(t.matmul(ps_bc[:, 512:1024], onesr_sb[:, :],
                                rw[:, 512:1024], start=True, stop=True))
            block.tensor(_t2)
            for mc in range(NMC):
                def _v3(v, mc=mc, tt=tt):
                    v.wait_ge(sT.h, sT.n)
                    v.tensor_mul(t1_sb[:, :], x1_t[:, 512 * mc:512 * (mc + 1)],
                                 ps_bc[:, 0:512])
                    v.drain()
                    sV.inc(v.tensor_add(
                        xn1_bf[:, SLAB * mc + 512 * tt:SLAB * mc + 512 * (tt + 1)],
                        t1_sb[:, :], ps_bc[:, 512:1024]))
                block.vector(_v3)
    fence()
    if "xn1" in dbg:
        def _d(s):
            for mc in range(NMC):
                dO.inc(s.dma_start(
                    out=dbg["xn1"].ap()[128 * mc:128 * (mc + 1), :],
                    in_=xn1_bf[:, SLAB * mc:SLAB * (mc + 1)]), 16)
        block.sync(_d)
    if upto == "E1a":
        _emit_zero_out(nc, block, sems, env["out_slab"])
        ctx_e.close()
        return

    def emit_x14(tt, x14, xn1f4, ps_bcw):
        def _s(s, tt=tt):
            s.wait_ge(sV.h, sV.n)
            for mc in range(4):
                if tt == 0:
                    dS.inc(s.dma_start(
                        out=x14[:, 514 * mc + 1:514 * mc + 514],
                        in_=x1spill.ap()[128 * mc:128 * (mc + 1), 0:513]), 16)
                elif tt == 3:
                    dS.inc(s.dma_start(
                        out=x14[:, 514 * mc:514 * mc + 513],
                        in_=x1spill.ap()[128 * mc:128 * (mc + 1),
                                         512 * tt - 1:SLAB]), 16)
                else:
                    dS.inc(s.dma_start(
                        out=x14[:, 514 * mc:514 * mc + 514],
                        in_=x1spill.ap()[128 * mc:128 * (mc + 1),
                                         512 * tt - 1:512 * (tt + 1) + 1]), 16)
        block.sync(_s)
        def _ve(v, tt=tt):
            v.wait_ge(dS.h, dS.n)
            if tt == 0:
                for mc in range(4):
                    v.tensor_copy(x14[:, 514 * mc:514 * mc + 1],
                                  x1h[:, mc:mc + 1])
            elif tt == 3:
                for mc in range(4):
                    v.tensor_copy(x14[:, 514 * mc + 513:514 * mc + 514],
                                  x1h[:, 8 + mc:8 + mc + 1])
            sV.inc(v.engine_nop())
        block.vector(_ve)
        def _t(t, tt=tt):
            t.wait_ge(sV.h, sV.n)
            t.matmul(ps_bcw[:, 0:512], onesr_sb[:, :],
                     rows_ln[0:1, 512 * tt:512 * tt + 512],
                     start=True, stop=True)
            t.matmul(ps_bcw[:, 512:514], onesr_sb[:, :],
                     rows_ln[0:1, 512 * tt + 512:512 * tt + 514],
                     start=True, stop=True)
            t.matmul(ps_bcw[:, 1024:1536], onesr_sb[:, :],
                     rows_ln[0:1, ROWB + 512 * tt:ROWB + 512 * tt + 512],
                     start=True, stop=True)
            sT.inc(t.matmul(ps_bcw[:, 1536:1538], onesr_sb[:, :],
                            rows_ln[0:1, ROWB + 512 * tt + 512:
                                    ROWB + 512 * tt + 514],
                            start=True, stop=True))
        block.tensor(_t)
        for mc in range(4):
            def _v(v, mc=mc):
                v.wait_ge(sT.h, sT.n)
                v.tensor_mul(xn1f4[:, 514 * mc:514 * (mc + 1)],
                             x14[:, 514 * mc:514 * (mc + 1)],
                             ps_bcw[:, 0:514])
                v.drain()
                sV.inc(v.tensor_add(xn1f4[:, 514 * mc:514 * (mc + 1)],
                                    xn1f4[:, 514 * mc:514 * (mc + 1)],
                                    ps_bcw[:, 1024:1538]))
            block.vector(_v)

    def emit_mixes(tt, xn1f4, branches, d_sb):
        for mc in range(4):
            xx_off = 0 if mc < 2 else 2
            for om_col, out4 in branches:
                def _v(v, mc=mc, xx_off=xx_off, om_col=om_col, out4=out4):
                    v.wait_ge(sV.h, sV.n)
                    v.tensor_sub(d_sb[:, :],
                                 xn1f4[:, 514 * mc + xx_off:
                                       514 * mc + xx_off + 512],
                                 xn1f4[:, 514 * mc + 1:514 * mc + 513])
                    v.drain()
                    sV.inc(v.scalar_tensor_tensor(
                        out4[:, 512 * mc:512 * (mc + 1)], d_sb[:, :],
                        om_col[:, mc:mc + 1],
                        xn1f4[:, 514 * mc + 1:514 * mc + 513],
                        op0=ALU.mult, op1=ALU.add))
                block.vector(_v)

    def kv_rhs(x4, tt, mc):
        if mc < 4:
            return x4[:, 512 * mc:512 * (mc + 1)]
        return xn1_bf[:, SLAB * mc + 512 * tt:SLAB * mc + 512 * (tt + 1)]

    with nc.sbuf_tensor("eb_x14", [128, 4 * 514], F32) as x14, \
         nc.sbuf_tensor("eb_xn", [128, 4 * 514], F32) as xn1f4, \
         nc.sbuf_tensor("eb_d", [128, 512], F32) as d_sb, \
         nc.sbuf_tensor("eb_xk", [128, 4 * 512], BF16) as xk4, \
         nc.sbuf_tensor("eb_xv", [128, 4 * 512], BF16) as xv4, \
         nc.sbuf_tensor("eb_ek", [128, 512], F32) as ek_sb, \
         nc.sbuf_tensor("eb_vf", [128, 512], F32) as vf_sb, \
         nc.sbuf_tensor("eb_za", [128, 512], F32) as za_sb, \
         nc.sbuf_tensor("eb_dmp", [128, 512], F32) as dump_sb, \
         nc.sbuf_tensor("eb_sum", [128, 16], F32) as sumst, \
         nc.psum_tensor("eb_bc", [128, 2048], F32) as ps_bcw, \
         nc.psum_tensor("eb_k", [128, 512], F32) as ps_k1, \
         nc.psum_tensor("eb_v", [128, 512], F32) as ps_v1:
        for tt in range(4):
            emit_x14(tt, x14, xn1f4, ps_bcw)
            emit_mixes(tt, xn1f4, [(omk_c, xk4), (omv_c, xv4)], d_sb)
            for oc in range(NMC):
                def _t(t, tt=tt, oc=oc):
                    t.wait_ge(sV.h, sV.n)
                    t.wait_ge(sA.h, sA.n)
                    for mc in range(NMC):
                        ins = t.matmul(ps_k1[:, :],
                                       wkT1[:, 1024 * mc + 128 * oc:
                                            1024 * mc + 128 * (oc + 1)],
                                       kv_rhs(xk4, tt, mc),
                                       start=(mc == 0), stop=(mc == NMC - 1))
                    sT.inc(ins)
                    for mc in range(NMC):
                        ins = t.matmul(ps_v1[:, :],
                                       wvT1[:, 1024 * mc + 128 * oc:
                                            1024 * mc + 128 * (oc + 1)],
                                       kv_rhs(xv4, tt, mc),
                                       start=(mc == 0), stop=(mc == NMC - 1))
                    sT.inc(ins)
                block.tensor(_t)
                def _a(a, oc=oc):
                    a.wait_ge(sT.h, sT.n)
                    a.wait_ge(sV.h, sV.n)
                    sA.inc(a.activation(ek_sb[:, :], ps_k1[:, :], AF.Exp))
                block.scalar(_a)
                def _v(v, tt=tt, oc=oc):
                    v.wait_ge(sA.h, sA.n)
                    v.tensor_copy(vf_sb[:, :], ps_v1[:, :])
                    v.tensor_copy(ewb[:, :],
                                  ew_c[:, oc:oc + 1].broadcast_to((128, 512)))
                    v.drain()
                    v.tensor_mul(za_sb[:, :], ek_sb[:, :], vf_sb[:, :])
                    v.drain()
                    v.tensor_tensor_scan(dump_sb[:, :], ewb[:, :], za_sb[:, :],
                                         initial=carry_a[:, oc:oc + 1],
                                         op0=ALU.mult, op1=ALU.add)
                    v.drain()
                    v.tensor_copy(carry_a[:, oc:oc + 1], dump_sb[:, 511:512])
                    v.tensor_tensor_scan(dump_sb[:, :], ewb[:, :], ek_sb[:, :],
                                         initial=carry_b[:, oc:oc + 1],
                                         op0=ALU.mult, op1=ALU.add)
                    v.drain()
                    sV.inc(v.tensor_copy(carry_b[:, oc:oc + 1],
                                         dump_sb[:, 511:512]))
                block.vector(_v)
        def _v(v):
            v.wait_ge(sV.h, sV.n)
            v.wait_ge(sA.h, sA.n)
            v.tensor_mul(sumst[:, 0:8], carry_a[:, :], gself[:, :])
            sV.inc(v.tensor_mul(sumst[:, 8:16], carry_b[:, :], gself[:, :]))
        block.vector(_v)
        def _s(s):
            s.wait_ge(sV.h, sV.n)
            with nc.allow_non_contiguous_dma(reason="tiny 8KB summary scatter"):
                dS.inc(s.dma_start(
                    out=bass.AP(ccs_in, 0, [[1, 128], [128, 8]]),
                    in_=sumst[:, 0:8], allow_slow_non_contiguous=True), 16)
                dS.inc(s.dma_start(
                    out=bass.AP(ccs_in, D, [[1, 128], [128, 8]]),
                    in_=sumst[:, 8:16], allow_slow_non_contiguous=True), 16)
        block.sync(_s)
        def _gp(gp):
            gp.wait_ge(dS.h, dS.n)
            cc.inc(gp.collective_compute(
                "AllGather", ALU.bypass, replica_groups=groups,
                ins=[ccs_in.ap()[:, :]], outs=[ccs_out.ap()[:, :]]), 1)
        block.gpsimd(_gp)

    with nc.sbuf_tensor("e2_cs", [8, D], F32) as cs_sb, \
         nc.sbuf_tensor("e2_ss", [128, 16], F32) as ssum, \
         nc.psum_tensor("e2_p", [128, 2], F32) as ps_s2:
        def _s(s):
            s.wait_ge(cc.h, 3)
            dS.inc(s.dma_start(out=cs_sb[:, :], in_=ccs_out.ap()[:, :]), 16)
        block.sync(_s)
        for oc in range(NMC):
            def _t(t, oc=oc):
                t.wait_ge(dS.h, dS.n)
                t.wait_ge(sV.h, sV.n)
                t.matmul(ps_s2[:, 0:1], cs_sb[:, 128 * oc:128 * (oc + 1)],
                         selA_sb[:, :], start=True, stop=True)
                sT.inc(t.matmul(ps_s2[:, 1:2],
                                cs_sb[:, 128 * oc:128 * (oc + 1)],
                                selB_sb[:, :], start=True, stop=True))
            block.tensor(_t)
            def _v(v, oc=oc):
                v.wait_ge(sT.h, sT.n)
                v.tensor_copy(ssum[:, oc:oc + 1], ps_s2[:, 0:1])
                sV.inc(v.tensor_copy(ssum[:, 8 + oc:8 + oc + 1], ps_s2[:, 1:2]))
            block.vector(_v)
        def _v2(v):
            v.wait_ge(sV.h, sV.n)
            v.tensor_mul(carry_a[:, :], gq_c[:, :], ssum[:, 0:8])
            sV.inc(v.tensor_mul(carry_b[:, :], gq_c[:, :], ssum[:, 8:16]))
        block.vector(_v2)
        if "sin" in dbg:
            def _ds(s):
                s.wait_ge(sV.h, sV.n)
                dO.inc(s.dma_start(out=dbg["sin"].ap()[:, 0:8],
                                   in_=carry_a[:, :]), 16)
                dO.inc(s.dma_start(out=dbg["sin"].ap()[:, 8:16],
                                   in_=carry_b[:, :]), 16)
            block.sync(_ds)
    fence()

    if upto == "E":
        _emit_zero_out(nc, block, sems, env["out_slab"])
        ctx_e.close()
        return

    _emit_g(nc, block, sems, env, dbg, ctx_e, locals())
    ctx_e.close()


def _emit_g(nc, block, sems, env, dbg, ctx_e, elocal):
    dS, dG, sT, sV, sA, sG, cc, dO = (sems[k] for k in
            ["dS", "dG", "sT", "sV", "sA", "sG", "cc", "dO"])
    g = lambda n: env[n]
    e = lambda n: elocal[n]
    ccol = env["ccol"]
    fence = env["fence"]
    fw1, fw2 = g("fw1"), g("fw2")
    x1spill, out_slab = g("x1spill"), g("out_slab")
    onesr_sb, ident_sb = g("onesr_sb"), g("ident_sb")
    b1f_c, b2f_c, u_c, ew_c = g("b1f_c"), g("b2f_c"), g("u_c"), g("ew_c")
    omk_c, omv_c, omr_c = e("omk_c"), e("omv_c"), e("omr_c")
    xn1_bf = e("xn1_bf")
    wkT1, wvT1, wrT1, woT1 = e("wkT1"), e("wvT1"), e("wrT1"), e("woT1")
    carry_a, carry_b = e("carry_a"), e("carry_b")
    ewb = e("ewb")
    emit_x14, emit_mixes, kv_rhs = e("emit_x14"), e("emit_mixes"), e("kv_rhs")

    terms = list(zip(PWL_BKS, PWL_DMS))
    act_terms = terms
    dve_terms = []
    gp_terms = []
    dve_const = -sum(dm * bk for bk, dm in dve_terms + gp_terms)

    ctx_g = ExitStack()
    SBG = lambda n, s, dt=F32: ctx_g.enter_context(nc.sbuf_tensor(n, s, dt))
    ffn_out = SBG("g_ffn", [128, NMC * 512])
    d_sb = SBG("g_d", [128, 512])
    x1r = SBG("g_x1r", [128, 512])
    ot_sb = SBG("g_ot", [128, 128])
    ps_f0 = ctx_g.enter_context(nc.psum_tensor("g_pf0", [128, 512], F32))
    ps_f1 = ctx_g.enter_context(nc.psum_tensor("g_pf1", [128, 512], F32))
    ps_bcw = ctx_g.enter_context(nc.psum_tensor("g_bc", [128, 2048], F32))
    ps_tr = ctx_g.enter_context(nc.psum_tensor("g_tr", [128, 128], F32))
    ps_f = [ps_f0, ps_f1]

    n_out_dma = 0
    for tt in range(4):
        fence()
        ctx_f = ExitStack()
        SBF = lambda n, s, dt=F32: ctx_f.enter_context(nc.sbuf_tensor(f"{n}_{tt}", s, dt))
        xg = SBF("g_xg", [128, 1024])
        tabs = SBF("g_tabs", [128, 1024])
        accp = SBF("g_acc", [128, 1024])
        rt = [SBF(f"g_rt{i}", [128, 1024]) for i in range(2)]
        accg = SBF("g_accg", [128, 1024])
        rtg = SBF("g_rtg", [128, 1024])
        h_bf = SBF("g_h", [128, 16 * 512], BF16)
        w1f = [SBF(f"g_w1f{i}", [128, 1024]) for i in range(2)]
        w1b = [SBF(f"g_w1b{i}", [128, 1024], BF16) for i in range(2)]
        w2f = [SBF(f"g_w2f{i}", [128, 1024]) for i in range(2)]
        w2b = [SBF(f"g_w2b{i}", [128, 1024], BF16) for i in range(2)]
        cntd = {}
        for rnd in range(2):
            for grp in range(8):
                for fc in range(2):
                    fcg = 16 * rnd + 2 * grp + fc
                    par = fcg % 2
                    def _s(s, fcg=fcg, par=par, tt=tt):
                        if cntd.get(("c1", fcg - 2)) is not None:
                            s.wait_ge(sG.h, cntd[("c1", fcg - 2)])
                        dS.inc(s.dma_start(
                            out=w1f[par][:, :].rearrange("p (mc c) -> p mc c", c=128),
                            in_=bass.AP(fw1, 128 * fcg,
                                        [[FF, 128], [128 * FF, NMC], [1, 128]])), 16)
                        cntd[("d1", fcg)] = dS.n
                    block.sync(_s)
                    def _gp(gp, fcg=fcg, par=par):
                        gp.wait_ge(dS.h, cntd[("d1", fcg)])
                        if cntd.get(("m1", fcg - 2)) is not None:
                            gp.wait_ge(sT.h, cntd[("m1", fcg - 2)])
                        sG.inc(gp.tensor_copy(w1b[par][:, :], w1f[par][:, :]))
                        cntd[("c1", fcg)] = sG.n
                    block.gpsimd(_gp)
                    def _t(t, fcg=fcg, par=par, tt=tt, fc=fc):
                        t.wait_ge(sG.h, cntd[("c1", fcg)])
                        if cntd.get(("a1", fcg - 2)) is not None:
                            t.wait_ge(sA.h, cntd[("a1", fcg - 2)])
                        for mc in range(NMC):
                            ins = t.matmul(
                                ps_f[par][:, :],
                                w1b[par][:, 128 * mc:128 * (mc + 1)],
                                xn1_bf[:, SLAB * mc + 512 * tt:
                                       SLAB * mc + 512 * (tt + 1)],
                                start=(mc == 0), stop=(mc == NMC - 1))
                        sT.inc(ins)
                        cntd[("m1", fcg)] = sT.n
                    block.tensor(_t)
                    def _a(a, fcg=fcg, par=par, fc=fc):
                        a.wait_ge(sT.h, cntd[("m1", fcg)])
                        a.wait_ge(sV.h, sV.n)
                        a.activation(rt[0][:, 512 * fc:512 * (fc + 1)],
                                     ps_f[par][:, :], AF.Identity, scale=256.0,
                                     bias=b1f_c[:, fcg:fcg + 1])
                        sA.inc(a.activation(xg[:, 512 * fc:512 * (fc + 1)],
                                            rt[0][:, 512 * fc:512 * (fc + 1)],
                                            AF.Identity, scale=1.0 / 256.0,
                                            bias=ccol(-C_MAGIC / 256.0)))
                        cntd[("a1", fcg)] = sA.n
                    block.scalar(_a)
                hslot = 2 * grp
                def _a2(a, grp=grp):
                    a.wait_ge(sV.h, sV.n)
                    sA.inc(a.activation(tabs[:, :], xg[:, :], AF.Abs))
                    cntd[("abs", grp)] = sA.n
                block.scalar(_a2)
                def _v(v, grp=grp):
                    v.wait_ge(sA.h, cntd[("abs", grp)])
                    sV.inc(v.tensor_scalar(accp[:, :], tabs[:, :], PWL_M0,
                                           dve_const, op0=ALU.mult,
                                           op1=ALU.add))
                    cntd[("base", grp)] = sV.n
                block.vector(_v)
                if gp_terms:
                    def _g4(gp, grp=grp):
                        gp.wait_ge(sA.h, cntd[("abs", grp)])
                        if ("gpc", grp - 1) in cntd:
                            gp.wait_ge(sV.h, cntd[("gpc", grp - 1)])
                        for gj, (bk, dm) in enumerate(gp_terms):
                            if gj == 0:
                                gp.tensor_scalar(accg[:, :], tabs[:, :], bk, dm,
                                                 op0=ALU.max, op1=ALU.mult)
                            else:
                                gp.tensor_scalar(rtg[:, :], tabs[:, :], bk, dm,
                                                 op0=ALU.max, op1=ALU.mult)
                                gp.drain()
                                gp.tensor_add(accg[:, :], accg[:, :], rtg[:, :])
                        sG.inc(gp.engine_nop())
                        cntd[("gp", grp)] = sG.n
                    block.gpsimd(_g4)
                na, nd = len(act_terms), len(dve_terms)
                for j in range(max(na, nd)):
                    if j < na:
                        bk, dm = act_terms[j]
                        def _ar(a, bk=bk, j=j, grp=grp):
                            if j == 0:
                                a.wait_ge(sV.h, cntd[("base", grp)])
                            elif ("stt", j - 2) in cntd:
                                a.wait_ge(sV.h, cntd[("stt", j - 2)])
                            sA.inc(a.activation(rt[j % 2][:, :], tabs[:, :],
                                                AF.Relu, bias=ccol(-bk)))
                            cntd[("r", j)] = sA.n
                        block.scalar(_ar)
                        def _vr(v, dm=dm, j=j):
                            v.wait_ge(sA.h, cntd[("r", j)])
                            sV.inc(v.scalar_tensor_tensor(
                                accp[:, :], rt[j % 2][:, :], dm, accp[:, :],
                                op0=ALU.mult, op1=ALU.add))
                            cntd[("stt", j)] = sV.n
                        block.vector(_vr)
                    if j < nd:
                        bk, dm = dve_terms[j]
                        def _vd2(v, bk=bk, dm=dm):
                            v.tensor_scalar(rt[2][:, :], tabs[:, :], bk, dm,
                                            op0=ALU.max, op1=ALU.mult)
                            v.drain()
                            sV.inc(v.tensor_add(accp[:, :], accp[:, :],
                                                rt[2][:, :]))
                        block.vector(_vd2)
                def _vf(v, grp=grp, hslot=hslot):
                    if gp_terms:
                        v.wait_ge(sG.h, cntd[("gp", grp)])
                        v.tensor_add(accp[:, :], accp[:, :], accg[:, :])
                    sV.inc(v.scalar_tensor_tensor(rt[1][:, :], xg[:, :], 0.5,
                                                  accp[:, :], op0=ALU.mult,
                                                  op1=ALU.add))
                    cntd[("y", grp)] = sV.n
                    cntd[("gpc", grp)] = sV.n
                block.vector(_vf)
                def _a3(a, grp=grp, hslot=hslot):
                    a.wait_ge(sV.h, cntd[("y", grp)])
                    a.activation(rt[0][:, :], rt[1][:, :], AF.Identity,
                                 scale=256.0, bias=ccol(C_MAGIC))
                    sA.inc(a.activation(
                        h_bf[:, 512 * hslot:512 * (hslot + 2)],
                        rt[0][:, :], AF.Identity, scale=1.0 / 256.0,
                        bias=ccol(-C_MAGIC / 256.0)))
                    cntd[("hf", grp)] = sA.n
                block.scalar(_a3)
            if "hfull" in dbg:
                def _dhf(s, tt=tt, rnd=rnd):
                    s.wait_ge(sA.h, sA.n)
                    for i in range(16):
                        dO.inc(s.dma_start(
                            out=dbg["hfull"].ap()[2048 * rnd + 128 * i:
                                                  2048 * rnd + 128 * (i + 1),
                                                  512 * tt:512 * (tt + 1)],
                            in_=h_bf[:, 512 * i:512 * (i + 1)]), 16)
                block.sync(_dhf)
            for dc in range(NMC):
                gi2 = 8 * rnd + dc
                for qq in range(2):
                    par = (2 * gi2 + qq) % 2
                    def _s(s, rnd=rnd, dc=dc, qq=qq, par=par, gi2=gi2):
                        if cntd.get(("c2", 2 * gi2 + qq - 2)) is not None:
                            s.wait_ge(sG.h, cntd[("c2", 2 * gi2 + qq - 2)])
                        dS.inc(s.dma_start(
                            out=w2f[par][:, :].rearrange("p (i c) -> p i c", c=128),
                            in_=bass.AP(fw2,
                                        (2048 * rnd + 1024 * qq) * D + 128 * dc,
                                        [[D, 128], [128 * D, 8], [1, 128]])), 16)
                        cntd[("d2", 2 * gi2 + qq)] = dS.n
                    block.sync(_s)
                    def _gp(gp, gi2=gi2, qq=qq, par=par):
                        gp.wait_ge(dS.h, cntd[("d2", 2 * gi2 + qq)])
                        if cntd.get(("m2", 2 * gi2 + qq - 2)) is not None:
                            gp.wait_ge(sT.h, cntd[("m2", 2 * gi2 + qq - 2)])
                        sG.inc(gp.tensor_copy(w2b[par][:, :], w2f[par][:, :]))
                        cntd[("c2", 2 * gi2 + qq)] = sG.n
                    block.gpsimd(_gp)
                    def _t(t, gi2=gi2, qq=qq, par=par, grpwait=(qq == 0 and dc == 0)):
                        t.wait_ge(sG.h, cntd[("c2", 2 * gi2 + qq)])
                        if grpwait:
                            t.wait_ge(sA.h, cntd[("hf", 7)])
                        if qq == 0 and cntd.get(("ev", gi2 - 1)) is not None:
                            t.wait_ge(sV.h, cntd[("ev", gi2 - 1)])
                        for i in range(8):
                            ins = t.matmul(
                                ps_f[gi2 % 2][:, :],
                                w2b[par][:, 128 * i:128 * (i + 1)],
                                h_bf[:, 512 * (8 * qq + i):512 * (8 * qq + i + 1)],
                                start=(qq == 0 and i == 0),
                                stop=(qq == 1 and i == 7))
                        sT.inc(ins)
                        cntd[("m2", 2 * gi2 + qq)] = sT.n
                    block.tensor(_t)
                def _v(v, rnd=rnd, dc=dc, gi2=gi2):
                    v.wait_ge(sT.h, cntd[("m2", 2 * gi2 + 1)])
                    if rnd == 0:
                        v.tensor_copy(ffn_out[:, 512 * dc:512 * (dc + 1)],
                                      ps_f[gi2 % 2][:, :])
                    else:
                        v.tensor_add(ffn_out[:, 512 * dc:512 * (dc + 1)],
                                     ffn_out[:, 512 * dc:512 * (dc + 1)],
                                     ps_f[gi2 % 2][:, :])
                    sV.inc(v.engine_nop())
                    cntd[("ev", gi2)] = sV.n
                block.vector(_v)
                if rnd == 1:
                    def _af(a, dc=dc, gi2=gi2):
                        a.wait_ge(sV.h, cntd[("ev", gi2)])
                        a.activation(rt[0][:, 0:512],
                                     ffn_out[:, 512 * dc:512 * (dc + 1)],
                                     AF.Identity, scale=256.0,
                                     bias=b2f_c[:, dc:dc + 1])
                        sA.inc(a.activation(
                            ffn_out[:, 512 * dc:512 * (dc + 1)],
                            rt[0][:, 0:512], AF.Identity, scale=1.0 / 256.0,
                            bias=ccol(-C_MAGIC / 256.0)))
                    block.scalar(_af)

        if "hff" in dbg:
            def _dh(s, tt=tt):
                s.wait_ge(sV.h, sV.n)
                for dc in range(NMC):
                    dO.inc(s.dma_start(
                        out=dbg["hff"].ap()[128 * dc:128 * (dc + 1),
                                            512 * tt:512 * (tt + 1)],
                        in_=ffn_out[:, 512 * dc:512 * (dc + 1)]), 16)
            block.sync(_dh)
        ctx_f.close()
        fence()
        ctx_s = ExitStack()
        SBS = lambda n, s, dt=F32: ctx_s.enter_context(nc.sbuf_tensor(f"{n}_{tt}", s, dt))
        x14 = SBS("g_x14", [128, 4 * 514])
        xn1f4 = SBS("g_xn", [128, 4 * 514])
        xk4 = SBS("g_xk", [128, 4 * 512], BF16)
        xv4 = SBS("g_xv", [128, 4 * 512], BF16)
        xr4 = SBS("g_xr", [128, 4 * 512], BF16)
        ek_sb = SBS("g_ek", [128, 512]); eku_sb = SBS("g_eku", [128, 512])
        vf_sb = SBS("g_vf", [128, 512]); za_sb = SBS("g_za", [128, 512])
        abuf = SBS("g_ab", [128, 513]); bbuf = SBS("g_bb", [128, 513])
        nf_sb = SBS("g_nf", [128, 512]); dinv_sb = SBS("g_di", [128, 512])
        wkv_sb = SBS("g_wkv", [128, 512]); sr_sb = SBS("g_sr", [128, 512])
        y_bf = SBS("g_y", [128, 8 * 512], BF16)
        emit_x14(tt, x14, xn1f4, ps_bcw)
        emit_mixes(tt, xn1f4, [(omk_c, xk4), (omv_c, xv4), (omr_c, xr4)], d_sb)
        if "xk" in dbg:
            def _dxk(s, tt=tt):
                s.wait_ge(sV.h, sV.n)
                for mc in range(4):
                    dO.inc(s.dma_start(
                        out=dbg["xk"].ap()[128 * mc:128 * (mc + 1),
                                           512 * tt:512 * (tt + 1)],
                        in_=xk4[:, 512 * mc:512 * (mc + 1)]), 16)
                    dO.inc(s.dma_start(
                        out=dbg["xnf"].ap()[128 * mc:128 * (mc + 1),
                                            512 * tt:512 * (tt + 1)],
                        in_=xn1f4[:, 514 * mc + 1:514 * mc + 513]), 16)
            block.sync(_dxk)
        for oc in range(NMC):
            def _t(t, tt=tt, oc=oc):
                t.wait_ge(sV.h, sV.n)
                t.wait_ge(sA.h, sA.n)
                for mc in range(NMC):
                    ins = t.matmul(ps_f0[:, :],
                                   wkT1[:, 1024 * mc + 128 * oc:
                                        1024 * mc + 128 * (oc + 1)],
                                   kv_rhs(xk4, tt, mc),
                                   start=(mc == 0), stop=(mc == NMC - 1))
                sT.inc(ins)
                for mc in range(NMC):
                    ins = t.matmul(ps_f1[:, :],
                                   wvT1[:, 1024 * mc + 128 * oc:
                                        1024 * mc + 128 * (oc + 1)],
                                   kv_rhs(xv4, tt, mc),
                                   start=(mc == 0), stop=(mc == NMC - 1))
                sT.inc(ins)
            block.tensor(_t)
            def _a(a, oc=oc):
                a.wait_ge(sT.h, sT.n)
                a.wait_ge(sV.h, sV.n)
                if "k2" in dbg:
                    a.activation(dinv_sb[:, :], ps_f0[:, :], AF.Identity)
                a.activation(ek_sb[:, :], ps_f0[:, :], AF.Exp)
                sA.inc(a.activation(eku_sb[:, :], ps_f0[:, :], AF.Exp,
                                    bias=u_c[:, oc:oc + 1]))
            block.scalar(_a)
            if "k2" in dbg:
                def _dk2(s, tt=tt, oc=oc):
                    s.wait_ge(sA.h, sA.n)
                    dO.inc(s.dma_start(
                        out=dbg["k2"].ap()[128 * oc:128 * (oc + 1),
                                           512 * tt:512 * (tt + 1)],
                        in_=dinv_sb[:, :]), 16)
                block.sync(_dk2)
            def _v(v, tt=tt, oc=oc):
                v.wait_ge(sA.h, sA.n)
                v.tensor_copy(vf_sb[:, :], ps_f1[:, :])
                v.tensor_copy(ewb[:, :],
                              ew_c[:, oc:oc + 1].broadcast_to((128, 512)))
                v.tensor_copy(abuf[:, 0:1], carry_a[:, oc:oc + 1])
                v.tensor_copy(bbuf[:, 0:1], carry_b[:, oc:oc + 1])
                v.drain()
                v.tensor_mul(za_sb[:, :], ek_sb[:, :], vf_sb[:, :])
                v.drain()
                v.tensor_tensor_scan(abuf[:, 1:513], ewb[:, :], za_sb[:, :],
                                     initial=carry_a[:, oc:oc + 1],
                                     op0=ALU.mult, op1=ALU.add)
                v.tensor_tensor_scan(bbuf[:, 1:513], ewb[:, :], ek_sb[:, :],
                                     initial=carry_b[:, oc:oc + 1],
                                     op0=ALU.mult, op1=ALU.add)
                v.drain()
                v.tensor_copy(carry_a[:, oc:oc + 1], abuf[:, 512:513])
                v.tensor_copy(carry_b[:, oc:oc + 1], bbuf[:, 512:513])
                v.tensor_mul(nf_sb[:, :], eku_sb[:, :], vf_sb[:, :])
                v.drain()
                v.tensor_add(nf_sb[:, :], nf_sb[:, :], abuf[:, 0:512])
                v.tensor_add(dinv_sb[:, :], bbuf[:, 0:512], eku_sb[:, :])
                v.drain()
                v.reciprocal(dinv_sb[:, :], dinv_sb[:, :])
                v.drain()
                sV.inc(v.tensor_mul(wkv_sb[:, :], nf_sb[:, :], dinv_sb[:, :]))
            block.vector(_v)
            def _t2(t, tt=tt, oc=oc):
                t.wait_ge(sV.h, sV.n)
                for mc in range(NMC):
                    ins = t.matmul(ps_f0[:, :],
                                   wrT1[:, 1024 * mc + 128 * oc:
                                        1024 * mc + 128 * (oc + 1)],
                                   kv_rhs(xr4, tt, mc),
                                   start=(mc == 0), stop=(mc == NMC - 1))
                sT.inc(ins)
            block.tensor(_t2)
            def _a2(a, oc=oc):
                a.wait_ge(sT.h, sT.n)
                sA.inc(a.activation(sr_sb[:, :], ps_f0[:, :], AF.Sigmoid))
            block.scalar(_a2)
            def _v2(v, oc=oc):
                v.wait_ge(sA.h, sA.n)
                sV.inc(v.tensor_mul(y_bf[:, 512 * oc:512 * (oc + 1)],
                                    sr_sb[:, :], wkv_sb[:, :]))
            block.vector(_v2)
            if "kv1" in dbg:
                def _dk(s, tt=tt, oc=oc):
                    s.wait_ge(sV.h, sV.n)
                    dO.inc(s.dma_start(
                        out=dbg["kv1"].ap()[128 * oc:128 * (oc + 1),
                                            512 * tt:512 * (tt + 1)],
                        in_=wkv_sb[:, :]), 16)
                block.sync(_dk)

        if "wkv" in dbg:
            def _d(s, tt=tt):
                s.wait_ge(sV.h, sV.n)
                for oc in range(NMC):
                    dO.inc(s.dma_start(
                        out=dbg["wkv"].ap()[128 * oc:128 * (oc + 1),
                                            512 * tt:512 * (tt + 1)],
                        in_=y_bf[:, 512 * oc:512 * (oc + 1)]), 16)
            block.sync(_d)

        for dc in range(NMC):
            def _t(t, dc=dc):
                t.wait_ge(sV.h, sV.n)
                for oc in range(NMC):
                    ins = t.matmul(ps_f1[:, :],
                                   woT1[:, 1024 * oc + 128 * dc:
                                        1024 * oc + 128 * (dc + 1)],
                                   y_bf[:, 512 * oc:512 * (oc + 1)],
                                   start=(oc == 0), stop=(oc == NMC - 1))
                sT.inc(ins)
            block.tensor(_t)
            def _s(s, dc=dc, tt=tt):
                s.wait_ge(sV.h, sV.n)
                dS.inc(s.dma_start(
                    out=x1r[:, :],
                    in_=x1spill.ap()[128 * dc:128 * (dc + 1),
                                     512 * tt:512 * (tt + 1)]), 16)
            block.sync(_s)
            def _v(v, dc=dc):
                v.wait_ge(sT.h, sT.n)
                v.wait_ge(dS.h, dS.n)
                v.wait_ge(sA.h, sA.n)
                v.tensor_add(d_sb[:, :], ffn_out[:, 512 * dc:512 * (dc + 1)],
                             ps_f1[:, :])
                v.drain()
                sV.inc(v.tensor_add(d_sb[:, :], d_sb[:, :], x1r[:, :]))
            block.vector(_v)
            for t1 in range(4):
                def _t2(t, t1=t1):
                    t.wait_ge(sV.h, sV.n)
                    sT.inc(t.transpose(ps_tr[:, :],
                                       d_sb[:, 128 * t1:128 * (t1 + 1)],
                                       ident_sb[:, :]))
                block.tensor(_t2)
                def _v2(v):
                    v.wait_ge(sT.h, sT.n)
                    v.wait_ge(dO.h, dO.n)
                    sV.inc(v.tensor_copy(ot_sb[:, :], ps_tr[:, :]))
                block.vector(_v2)
                def _s2(s, tt=tt, dc=dc, t1=t1):
                    s.wait_ge(sV.h, sV.n)
                    dO.inc(s.dma_start(
                        out=out_slab.ap()[512 * tt + 128 * t1:
                                          512 * tt + 128 * (t1 + 1),
                                          128 * dc:128 * (dc + 1)],
                        in_=ot_sb[:, :]), 16)
                block.sync(_s2)
                n_out_dma += 1
        ctx_s.close()

    def _fin(s):
        for sem in sems.values():
            if sem.n > 0:
                s.wait_ge(sem.h, sem.n)
    block.sync(_fin)
    ctx_g.close()


def _stage_inputs(inputs):
    x = np.ascontiguousarray(np.asarray(inputs["L0_attn_x"], np.float32))
    ident, ones_col, ones_row, masks_bf = _aux_arrays()
    com = {
        "wo0": np.ascontiguousarray(np.asarray(inputs["L0_attn_Wo"], np.float32)),
        "wg": np.ascontiguousarray(np.asarray(inputs["L0_ffn_Wg"], np.float32)),
        "bg_": np.ascontiguousarray(np.asarray(inputs["L0_ffn_bg"], np.float32).reshape(8, 1)),
        "w1moe": np.ascontiguousarray(np.asarray(inputs["L0_ffn_W1"], np.float32)),
        "b1moe": np.ascontiguousarray(np.asarray(inputs["L0_ffn_b1"], np.float32).reshape(8, 32, 128)),
        "w2moe": np.ascontiguousarray(np.asarray(inputs["L0_ffn_W2"], np.float32)),
        "b2moe": np.ascontiguousarray(np.asarray(inputs["L0_ffn_b2"], np.float32)),
        "l1wk": np.ascontiguousarray(np.asarray(inputs["L1_attn_Wk"], np.float32)),
        "l1wv": np.ascontiguousarray(np.asarray(inputs["L1_attn_Wv"], np.float32)),
        "l1wr": np.ascontiguousarray(np.asarray(inputs["L1_attn_Wr"], np.float32)),
        "l1wo": np.ascontiguousarray(np.asarray(inputs["L1_attn_Wo"], np.float32)),
        "sdec": np.ascontiguousarray(np.asarray(inputs["L1_attn_spatial_decay"], np.float32).reshape(8, 128)),
        "sfirst": np.ascontiguousarray(np.asarray(inputs["L1_attn_spatial_first"], np.float32).reshape(8, 128)),
        "mixk": np.ascontiguousarray(np.asarray(inputs["L1_attn_mix_k"], np.float32).reshape(8, 128)),
        "mixv": np.ascontiguousarray(np.asarray(inputs["L1_attn_mix_v"], np.float32).reshape(8, 128)),
        "mixr": np.ascontiguousarray(np.asarray(inputs["L1_attn_mix_r"], np.float32).reshape(8, 128)),
        "fw1": np.ascontiguousarray(np.asarray(inputs["L1_ffn_w1"], np.float32)),
        "fb1": np.ascontiguousarray(np.asarray(inputs["L1_ffn_b1"], np.float32).reshape(32, 128)),
        "fw2": np.ascontiguousarray(np.asarray(inputs["L1_ffn_w2"], np.float32)),
        "fb2": np.ascontiguousarray(np.asarray(inputs["L1_ffn_b2"], np.float32).reshape(8, 128)),
        "ident": ident, "onesc": ones_col, "onesr": ones_row,
        "masks": masks_bf,
        "esel": np.eye(8, dtype=np.float32).repeat(128, axis=1).reshape(8, 8, 128).reshape(8, 8 * 128),
    }
    wq = np.asarray(inputs["L0_attn_Wq"], np.float32)
    wk = np.asarray(inputs["L0_attn_Wk"], np.float32)
    wv = np.asarray(inputs["L0_attn_Wv"], np.float32)
    in_maps = []
    for c in range(NCORE):
        s, q = c // 4, c % 4
        selL = np.zeros((8, 1), np.float32)
        selR = np.zeros((8, 1), np.float32)
        if q > 0:
            selL[2 * (q - 1) + 1, 0] = 1.0
        if q < 3:
            selR[2 * (q + 1) + 0, 0] = 1.0
        selA = np.zeros((8, 1), np.float32)
        selB = np.zeros((8, 1), np.float32)
        for i in range(q):
            selA[2 * i, 0] = 1.0
            selB[2 * i + 1, 0] = 1.0
        m = dict(com)
        m["xseq"] = np.ascontiguousarray(x[s])
        m["xslab"] = np.ascontiguousarray(x[s, SLAB * q:SLAB * (q + 1)])
        m["wq_my"] = np.ascontiguousarray(wq[256 * q:256 * (q + 1)])
        m["wk_my"] = np.ascontiguousarray(wk[256 * q:256 * (q + 1)])
        m["wv_my"] = np.ascontiguousarray(wv[256 * q:256 * (q + 1)])
        m["selL"] = selL; m["selR"] = selR; m["selA"] = selA; m["selB"] = selB
        m["qpow"] = np.array([[float(q)]], np.float32)
        m["spow"] = np.array([[-float(q + 1)]], np.float32)
        m["soff"] = np.array([[SLAB * q]], np.int32)
        in_maps.append(m)
    return in_maps


_CACHE = {}

def _run(inputs, upto="full", debug=()):
    key = (upto, tuple(sorted(debug)))
    if key not in _CACHE:
        _CACHE[key] = build_nc(upto=upto, debug=debug)
    nc, dbg = _CACHE[key]
    in_maps = _stage_inputs(inputs)
    return run_bass_kernel_spmd(nc, in_maps, core_ids=list(range(NCORE)))


def kernel(**inputs):
    res = _run(inputs, upto="full")
    out = np.empty((B, T, D), np.float32)
    for c in range(NCORE):
        s, q = c // 4, c % 4
        out[s, SLAB * q:SLAB * (q + 1)] = np.asarray(res.results[c]["out_slab"])
    return out
